# version 09
# baseline (speedup 1.0000x reference)
import math
import numpy as np
import ml_dtypes
import contextlib

import concourse.bass as bass
import concourse.mybir as mybir
from concourse.bass_utils import run_bass_kernel_spmd

F32 = mybir.dt.float32
BF16 = mybir.dt.bfloat16
AF = mybir.ActivationFunctionType
ALU = mybir.AluOpType
AX = mybir.AxisListType
BF = ml_dtypes.bfloat16

N_CORES = 8


def full_cfg():
    return dict(B=4, D=2048, H=16, KV=4, HD=128, T_own=2048, T_kv=4096,
                E=32, F=768, SF=5632, PF=512, TC=1024, K_top=4,
                eps=1e-5, theta=1e4)


def mini_cfg():
    return dict(B=2, D=256, H=2, KV=1, HD=128, T_own=256, T_kv=512,
                E=8, F=256, SF=512, PF=256, TC=128, K_top=4,
                eps=1e-5, theta=1e4)


def derived(cfg):
    d = dict(cfg)
    d["DC"] = cfg["D"] // 128
    d["KVD"] = cfg["KV"] * cfg["HD"]
    d["KVC"] = d["KVD"] // 128
    d["G"] = cfg["H"] // cfg["KV"]
    d["nqt"] = cfg["T_own"] // 128
    d["nst"] = cfg["T_kv"] // 128
    d["FC"] = cfg["F"] // 128
    d["PFC"] = cfg["PF"] // 128
    d["NPSE"] = cfg["SF"] // cfg["PF"]
    d["scale"] = cfg["HD"] ** -0.5
    d["QG"] = min(1024, cfg["T_own"])
    d["NSUB"] = min(512, d["QG"])
    d["nch"] = cfg["T_own"] // cfg["TC"]
    d["ntt"] = cfg["TC"] // 128
    d["NF"] = min(512, cfg["TC"])
    return d



def rope_tables(cfg):
    HD = cfg["HD"]
    inv = 1.0 / (cfg["theta"] ** (np.arange(0, HD, 2, dtype=np.float64) / HD))

    def flat(nh):
        ang = np.arange(nh, dtype=np.float64)[:, None] * inv[None, :]
        cos = np.cos(ang)
        sin = np.sin(ang)
        cosf = np.concatenate([cos, cos], axis=1).reshape(-1)
        sinf = np.concatenate([-sin, sin], axis=1).reshape(-1)
        return cosf.astype(np.float32), sinf.astype(np.float32)

    cq, sq = flat(cfg["H"])
    ck, sk = flat(cfg["KV"])
    return cq, sq, ck, sk


def host_constants(cfg):
    d = derived(cfg)
    cq, sq, ck, sk = rope_tables(cfg)
    return {
        "ident_bf": np.eye(128, dtype=BF),
        "ident_f32": np.eye(128, dtype=np.float32),
        "tril_os": np.tril(np.full((128, 128), 1.0 / d["scale"],
                                   dtype=np.float32)).astype(BF),
        "cosq_rep": np.broadcast_to(cq.astype(BF), (128, cfg["D"])).copy(),
        "sinq_rep": np.broadcast_to(sq.astype(BF), (128, cfg["D"])).copy(),
        "cosk_rep": np.broadcast_to(ck.astype(BF), (128, d["KVD"])).copy(),
        "sink_rep": np.broadcast_to(sk.astype(BF), (128, d["KVD"])).copy(),
        "cosq16": cq.reshape(cfg["H"], cfg["HD"]).copy(),
        "sinq16": sq.reshape(cfg["H"], cfg["HD"]).copy(),
        "cosk16": ck.reshape(cfg["KV"], cfg["HD"]).copy(),
        "sink16": sk.reshape(cfg["KV"], cfg["HD"]).copy(),
    }


def prep_core_inputs(inputs, core, cfg):
    d = derived(cfg)
    b = core // 2
    half = core % 2
    x = np.asarray(inputs["L0_attn_x"])
    x_kv = np.roll(x[b], -half * cfg["T_own"], axis=0)
    cm = {
        "x_kv": np.ascontiguousarray(x_kv, dtype=np.float32),
        "wqT": np.ascontiguousarray(np.asarray(inputs["L0_attn_wq"]).T),
        "wkT": np.ascontiguousarray(np.asarray(inputs["L0_attn_wk"]).T),
        "wvT": np.ascontiguousarray(np.asarray(inputs["L0_attn_wv"]).T),
        "woT": np.ascontiguousarray(np.asarray(inputs["L0_attn_wo"]).T),
        "bq": np.asarray(inputs["L0_attn_bq"]).reshape(1, -1).copy(),
        "bk": np.asarray(inputs["L0_attn_bk"]).reshape(1, -1).copy(),
        "bv": np.asarray(inputs["L0_attn_bv"]).reshape(1, -1).copy(),
        "bo": np.asarray(inputs["L0_attn_bo"]).reshape(1, -1).copy(),
        "gseg": np.ascontiguousarray(np.concatenate(
            [np.asarray(inputs["L0_ffn_gate_w"]),
             np.asarray(inputs["L0_ffn_seg"])], axis=1)),
        "wg": np.ascontiguousarray(inputs["L0_ffn_wg"]),
        "wu": np.ascontiguousarray(inputs["L0_ffn_wu"]),
        "wd": np.ascontiguousarray(inputs["L0_ffn_wd"]),
        "sgb": np.ascontiguousarray(
            np.asarray(inputs["L0_ffn_sg"]).reshape(cfg["D"], d["NPSE"], cfg["PF"])
            .transpose(1, 0, 2)),
        "sub": np.ascontiguousarray(
            np.asarray(inputs["L0_ffn_su"]).reshape(cfg["D"], d["NPSE"], cfg["PF"])
            .transpose(1, 0, 2)),
        "sd": np.ascontiguousarray(inputs["L0_ffn_sd"]),
        "hflag": np.full((128, 1), float(half), dtype=np.float32),
    }
    cm.update(host_constants(cfg))
    return cm



ENGS = ("sync", "scalar", "vector", "tensor", "gpsimd")


class S:
    __slots__ = ("h", "n", "name")

    def __init__(self, h, name):
        self.h = h
        self.n = 0
        self.name = name

    @property
    def ev(self):
        return (self, self.n)


class Plan:
    def __init__(self, nc, es):
        self.nc = nc
        self.items = {e: [] for e in ENGS}
        self.es = es
        self.all_sems = []

    def sem(self, name):
        s = S(self.es.enter_context(self.nc.semaphore(name)), name)
        self.all_sems.append(s)
        return s

    def phase_barrier(self):
        evs = tuple((s, s.n) for s in self.all_sems if s.n > 0)
        for eng in ENGS:
            self.emit(eng, lambda e: e.nop(), waits=evs)

    def emit(self, eng, fn, waits=(), incs=()):
        for s, v in waits:
            assert 0 <= v <= s.n, f"wait on future value of {s.name}: {v} > {s.n}"
        for s, k in incs:
            s.n += k
        self.items[eng].append((tuple(waits), fn, tuple(incs)))

    def dma(self, eng, out_ap, in_ap, sem, waits=()):
        self.emit(eng, lambda e, o=out_ap, i=in_ap: e.dma_start(o, i),
                  waits=waits, incs=((sem, 16),))
        return sem.ev

    def replay(self):
        nc = self.nc
        with nc.Block() as block:
            for engname in ENGS:
                items = self.items[engname]
                if not items:
                    continue

                def make(its):
                    def _(eng):
                        seen = {}
                        for waits, fn, incs in its:
                            for s, v in waits:
                                if v > 0 and seen.get(id(s), -1) < v:
                                    eng.wait_ge(s.h, v)
                                    seen[id(s)] = v
                            inst = fn(eng)
                            for s, k in incs:
                                inst.then_inc(s.h, k)
                    return _

                getattr(block, engname)(make(items))



def build_program(cfg, debug=(), BassCls=None, stop_after=None, ph3_skip=(), ph1c_skip=()):
    d = derived(cfg)
    DC, KVC, KVD = d["DC"], d["KVC"], d["KVD"]
    H, KV, G, HD = cfg["H"], cfg["KV"], d["G"], cfg["HD"]
    D, T_own, T_kv = cfg["D"], cfg["T_own"], cfg["T_kv"]
    E, F, FC = cfg["E"], cfg["F"], d["FC"]
    SF, PF, PFC, NPSE = cfg["SF"], cfg["PF"], d["PFC"], d["NPSE"]
    TC, ntt, NF, nch = cfg["TC"], d["ntt"], d["NF"], d["nch"]
    nqt, nst = d["nqt"], d["nst"]
    scale, eps = d["scale"], cfg["eps"]
    half = HD // 2
    QG = min(1024, T_own)
    NSUB = min(512, QG)
    nsub = QG // NSUB
    QGK = min(256, T_kv)
    nfsub = TC // NF
    FOG = max(1, D // 512)
    FW = min(512, D)
    hD = DC // 2
    EP1 = E + 1
    Ktop = cfg["K_top"]

    if BassCls is None:
        BassCls = bass.Bass
    nc = BassCls("TRN2", target_bir_lowering=False)

    dr = {}

    def din(name, shape, dt=F32):
        dr[name] = nc.dram_tensor(name, list(shape), dt, kind="ExternalInput").ap()

    def dout(name, shape, dt=F32):
        dr[name] = nc.dram_tensor(name, list(shape), dt, kind="ExternalOutput").ap()

    din("x_kv", (T_kv, D))
    din("wqT", (D, D)); din("wkT", (D, KVD)); din("wvT", (D, KVD)); din("woT", (D, D))
    din("bq", (1, D)); din("bk", (1, KVD)); din("bv", (1, KVD)); din("bo", (1, D))
    din("gseg", (D, E + 1))
    din("wg", (E, D, F)); din("wu", (E, D, F)); din("wd", (E, F, D))
    din("sgb", (NPSE, D, PF)); din("sub", (NPSE, D, PF)); din("sd", (SF, D))
    din("hflag", (128, 1))
    din("ident_bf", (128, 128), BF16); din("ident_f32", (128, 128))
    din("tril_os", (128, 128), BF16)
    din("cosq_rep", (128, D), BF16); din("sinq_rep", (128, D), BF16)
    din("cosk_rep", (128, KVD), BF16); din("sink_rep", (128, KVD), BF16)
    din("cosq16", (H, HD)); din("sinq16", (H, HD))
    din("cosk16", (KV, HD)); din("sink16", (KV, HD))
    dout("out", (T_own, D))
    dr["x2sp"] = nc.dram_tensor("x2sp", [T_own, D], F32, kind="Internal").ap()
    if nch > 1:
        dr["x2fmsp"] = nc.dram_tensor("x2fmsp", [128, (nch - 1) * DC * TC], BF16,
                                      kind="Internal").ap()

    es = contextlib.ExitStack()
    P = Plan(nc, es)
    s_const = P.sem("s_const")
    s_wq = P.sem("s_wq")
    s_wk = P.sem("s_wk")
    s_wo = P.sem("s_wo")
    s_repq = P.sem("s_repq")
    s_repk = P.sem("s_repk")
    s_x = P.sem("s_x")
    s_xr = P.sem("s_xr")
    s_sp = [P.sem("s_sp0"), P.sem("s_sp1")]
    s_x2r = P.sem("s_x2r")
    s_x2f = P.sem("s_x2f")
    s_x2l = [P.sem("s_x2l0"), P.sem("s_x2l1")]
    s_st = [P.sem("s_st0"), P.sem("s_st1")]
    s_dbg = P.sem("s_dbg")
    NSLOT = 6
    s_w = [P.sem("s_w%d" % i) for i in range(NSLOT)]
    s_pe = P.sem("s_pe")
    s_act = P.sem("s_act")
    s_dve = P.sem("s_dve")
    s_pl = P.sem("s_pl")

    SLOTW = max((DC // 2) * F, ((FC + 1) // 2) * D,
                (DC // 2) * PF, ((PFC + 1) // 2) * D)
    S0 = max(H * T_own * 2, ntt * D * 4)
    S1 = max(KVC * T_kv * 2 + nst * KVD * 2, DC * D * 2 + D * 4,
             NSLOT * SLOTW * 2, 4 * D * 4 + D * 2)
    S2 = max(DC * QG * 2, G * T_kv * 2, DC * TC * 2,
             DC * QGK * 2 + DC * KVD * 2 + (QGK // 128) * D * 2)
    NRW = min(2048, T_kv)
    S3 = max(10 * D,
             DC * KVD * 2 + 6 * KVD,
             6 * max(T_own, NRW),
             4 * T_kv,
             max(FC, PFC) * TC * 2 + 2 * D + 2 * TC)
    RTOP = nc._sbuf_addr_for_side("right")
    AB = RTOP - (S0 + S1 + S2 + S3)
    R0, R1, R2, R3 = AB, AB + S0, AB + S0 + S1, AB + S0 + S1 + S2

    def at(name, shape, dt, off):
        return nc.alloc_sbuf_tensor_at(name, list(shape), dt, offset=off)

    def sb(name, shape, dt, sc=es):
        return sc.enter_context(nc.sbuf_tensor(name, list(shape), dt))

    def ps(name, shape, dt, sc=es):
        return sc.enter_context(nc.psum_tensor(name, list(shape), dt))

    q_P = at("q_P", (128, H * T_own), BF16, R0)
    k_P = at("k_P", (128, KVC * T_kv), BF16, R1)
    v_T = at("v_T", (128, nst * KVD), BF16, R1 + KVC * T_kv * 2)
    ident_b = sb("ident_b", (128, 128), BF16)
    ident_f = sb("ident_f", (128, 128), F32)
    tril = sb("tril", (128, 128), BF16)
    hflag = sb("hflag_s", (128, 1), F32)
    ones_b = sb("ones_b", (1, 512), BF16)
    ones_c = sb("ones_c", (128, 1), BF16)
    eps_c = sb("eps_c", (128, 1), F32)
    gseg_s = sb("gseg_s", (128, DC * EP1), F32)
    ltma = sb("ltma", (128, nqt * EP1), F32)
    bounds_c = sb("bounds_c", (128, H * nqt), F32)
    cs16 = sb("cs16", (max(H, 2), 4 * HD), F32)
    b16 = sb("b16", (max(H, 2), 4 * HD), F32)
    bqT = sb("bqT", (128, DC + KVC), F32)
    bv_row = sb("bv_row", (1, KVD), BF16)
    bo_row = sb("bo_row", (1, D), BF16)

    for nm, t in (("ident_bf", ident_b), ("ident_f32", ident_f),
                  ("tril_os", tril), ("hflag", hflag)):
        P.dma("sync", t[:, :], dr[nm], s_const)
    for fi in range(DC):
        P.dma("sync", gseg_s[:, fi * EP1:(fi + 1) * EP1],
              dr["gseg"][fi * 128:(fi + 1) * 128, :], s_const)
    P.dma("sync", cs16[0:H, 0:HD], dr["cosq16"], s_const)
    P.dma("sync", cs16[0:H, HD:2 * HD], dr["sinq16"], s_const)
    P.dma("sync", cs16[0:KV, 2 * HD:3 * HD], dr["cosk16"], s_const)
    P.dma("sync", cs16[0:KV, 3 * HD:4 * HD], dr["sink16"], s_const)
    P.dma("gpsimd", bv_row[:, :], dr["bv"], s_dbg)
    P.dma("gpsimd", bo_row[:, :], dr["bo"], s_dbg)
    ev_brows = (s_dbg, s_dbg.n)
    P.emit("vector", lambda e: e.memset(ones_b[:, :], 1.0))
    P.emit("vector", lambda e: e.memset(ones_c[:, :], 1.0))
    P.emit("vector", lambda e: e.memset(eps_c[:, :], float(eps)),
           incs=((s_dve, 1),))
    ev_ones = s_dve.ev
    P.dma("sync", b16[0:DC, 0:HD],
          bass.AP(dr["bq"].tensor, 0, [[128, DC], [1, 128]]), s_const)
    P.dma("sync", b16[0:KVC, HD:2 * HD],
          bass.AP(dr["bk"].tensor, 0, [[128, KVC], [1, 128]]), s_const)
    ev_consts = (s_const, s_const.n)

    pbt_scope = contextlib.ExitStack()
    pbt = ps("pbt", (128, DC + KVC), F32, pbt_scope)

    def bias_cols(src_c0, C, coff, soff, dst_off, ev_prev):
        s_ = b16[0:C, src_c0:src_c0 + HD]
        P.emit("vector", lambda e: e.tensor_mul(
            b16[0:C, 2 * HD:2 * HD + half], s_[0:C, half:HD],
            cs16[0:C, soff:soff + half]),
            waits=(ev_consts,) + ev_prev, incs=((s_dve, 1),))
        P.emit("vector", lambda e: e.tensor_mul(
            b16[0:C, 2 * HD + half:3 * HD], s_[0:C, 0:half],
            cs16[0:C, soff + half:soff + HD]),
            incs=((s_dve, 1),))
        P.emit("vector", lambda e: e.tensor_mul(
            b16[0:C, 3 * HD:4 * HD], s_, cs16[0:C, coff:coff + HD]),
            incs=((s_dve, 1),))
        ev_m3 = s_dve.ev
        P.emit("vector", lambda e: e.tensor_add(
            s_, b16[0:C, 2 * HD:3 * HD], b16[0:C, 3 * HD:4 * HD]),
            waits=(ev_m3,), incs=((s_dve, 1),))
        ev_r = s_dve.ev
        P.emit("tensor", lambda e, C=C: e.transpose(
            pbt[:, dst_off:dst_off + C], s_, ident_f[0:C, 0:C]),
            waits=(ev_r, ev_consts), incs=((s_pe, 1),))
        ev_t = s_pe.ev
        P.emit("vector", lambda e, C=C: e.tensor_copy(
            bqT[:, dst_off:dst_off + C], pbt[:, dst_off:dst_off + C]),
            waits=(ev_t,), incs=((s_dve, 1),))
        return s_dve.ev

    def _stop():
        P.replay()
        es.close()
        return nc, dr

    ev_bq = bias_cols(0, DC, 0, HD, 0, ())
    ev_bias_cols = bias_cols(HD, KVC, 2 * HD, 3 * HD, DC, (ev_bq,))
    pbt_scope.close()
    P.phase_barrier()
    if stop_after == "consts":
        return _stop()

    def x_rows_ap(r0, ntiles):
        return bass.AP(dr["x_kv"].tensor, r0 * D,
                       [[D, 128], [128 * D, ntiles], [1, D]])

    def wslab_ap(dname, fi, width):
        return dr[dname][fi * 128:(fi + 1) * 128, :]

    def rot_weight(ws, rep, rscr, width, nheads, ev_load, ev_rep, ev_prev):
        w0 = (ev_load, ev_rep) + (ev_prev if ev_prev else ())
        for h in range(nheads):
            P.emit("vector", lambda e, h=h: e.tensor_mul(
                rscr[:, h * HD:h * HD + half],
                ws[:, h * HD + half:h * HD + HD],
                rep[:, width + h * HD:width + h * HD + half]),
                waits=w0 if h == 0 else ())
            P.emit("vector", lambda e, h=h: e.tensor_mul(
                rscr[:, h * HD + half:h * HD + HD],
                ws[:, h * HD:h * HD + half],
                rep[:, width + h * HD + half:width + h * HD + HD]),
                incs=((s_dve, 1),) if h == nheads - 1 else ())
        ev_sin = s_dve.ev
        P.emit("vector", lambda e: e.tensor_mul(ws, ws, rep[:, 0:width]),
               waits=(ev_sin,), incs=((s_dve, 1),))
        evc = s_dve.ev
        P.emit("vector", lambda e: e.tensor_add(ws, ws, rscr[:, 0:width]),
               waits=(evc,), incs=((s_dve, 1),))
        return s_dve.ev

    wq_s = at("wq_s", (128, DC * D), BF16, R1)
    xfm = at("xfm", (128, DC * QG), BF16, R2)
    xtm = at("xtm", (128, min(QG // 128, 2) * D), BF16, R3)
    repq = at("repq", (128, 2 * D), BF16, R3 + min(QG // 128, 2) * D * 2)
    rot_s = at("rot_s", (128, D), BF16, R3 + min(QG // 128, 2) * D * 2 + 4 * D)
    ph1 = contextlib.ExitStack()
    ptrA = ps("ptrA", (128, 1024), BF16, ph1)
    ptrB = ps("ptrB", (128, 1024), BF16, ph1)
    pq0 = ps("pq0", (128, QG), F32, ph1)
    pq1 = ps("pq1", (128, QG), F32, ph1)

    P.dma("sync", repq[:, 0:D], dr["cosq_rep"], s_repq)
    P.dma("sync", repq[:, D:2 * D], dr["sinq_rep"], s_repq)
    ev_repq = (s_repq, s_repq.n)
    for fi in range(DC):
        P.dma("gpsimd", wq_s[:, fi * D:(fi + 1) * D], wslab_ap("wqT", fi, D), s_wq)
    ev_wq_all = (s_wq, s_wq.n)
    ev_wq_rot = None
    for fi in range(DC):
        ev_wq_rot = rot_weight(wq_s[:, fi * D:(fi + 1) * D], repq, rot_s,
                               D, H, ev_wq_all, ev_repq,
                               (ev_wq_rot,) if ev_wq_rot else None)

    def load_transpose_group(xtm_t, xfm_t, ptA, ptB, r0, ntok, width_cols,
                             ev_prev_mms, ev_dma_first=()):
        nhalf = xtm_t.shape[1] // D
        ev_ev = {0: None, 1: None}
        ev_dma_prev = tuple(ev_dma_first)
        ev_last_tp = None
        for hf in range(ntok // (nhalf * 128)):
            hr0 = r0 + hf * nhalf * 128
            evd = P.dma("gpsimd", xtm_t[:, :], x_rows_ap(hr0, nhalf), s_x,
                        waits=ev_dma_prev)
            for fi in range(DC):
                pp = fi % 2
                ptr = ptA if pp == 0 else ptB
                w0 = [evd, ev_consts]
                if ev_ev[pp] is not None:
                    w0.append(ev_ev[pp])
                if fi == 0 and ev_prev_mms:
                    w0.extend(ev_prev_mms)
                for tt in range(nhalf):
                    P.emit("tensor", lambda e, fi=fi, tt=tt, ptr=ptr, hf=hf,
                           xt=xtm_t: e.transpose(
                        ptr[:, tt * 128:(tt + 1) * 128],
                        xt[:, tt * D + fi * 128:tt * D + (fi + 1) * 128],
                        ident_b[:, :]),
                        waits=tuple(w0) if tt == 0 else (),
                        incs=((s_pe, 1),) if tt == nhalf - 1 else ())
                ev_t = s_pe.ev
                ev_last_tp = ev_t
                P.emit("vector", lambda e, fi=fi, ptr=ptr, hf=hf, nh=nhalf,
                       xf=xfm_t: e.tensor_copy(
                    xf[:, fi * width_cols + hf * nh * 128:
                       fi * width_cols + (hf + 1) * nh * 128],
                    ptr[:, 0:nh * 128]),
                    waits=(ev_t,), incs=((s_dve, 1),))
                ev_ev[pp] = s_dve.ev
            ev_dma_prev = (ev_last_tp,)
        return s_dve.ev, ev_last_tp

    ev_prevq = ()
    ev_tp_q = ()
    for g in range(T_own // QG):
        ev_xfm, ev_ltp = load_transpose_group(xtm, xfm, ptrA, ptrB, g * QG, QG,
                                              QG, ev_prevq, ev_tp_q)
        ev_tp_q = (ev_ltp,)
        ev_evq = {0: None, 1: None}
        for pr in range(DC // 2):
            for j, pq in ((0, pq0), (1, pq1)):
                fo = 2 * pr + j
                w = [ev_xfm, ev_wq_rot, ev_ones]
                if ev_evq[j] is not None:
                    w.append(ev_evq[j])
                for fi in range(DC):
                    for ns in range(nsub):
                        stq = (fi == 0) and (ns == 0 or (ns * NSUB * 4) % 2048 == 0)
                        P.emit("tensor", lambda e, fi=fi, fo=fo, ns=ns, pq=pq,
                               stq=stq: e.matmul(
                            pq[:, ns * NSUB:(ns + 1) * NSUB],
                            wq_s[:, fi * D + fo * 128:fi * D + (fo + 1) * 128],
                            xfm[:, fi * QG + ns * NSUB:fi * QG + (ns + 1) * NSUB],
                            start=stq, stop=(fi == DC - 1),
                            skip_group_check=True),
                            waits=tuple(w) if fi == 0 and ns == 0 else (),
                            incs=((s_pe, 1),)
                                 if fi == DC - 1 and ns == nsub - 1 else ())
                ev_mm = s_pe.ev
                P.emit("scalar", lambda e, fo=fo, g=g, pq=pq: e.activation(
                    q_P[:, fo * T_own + g * QG:fo * T_own + (g + 1) * QG],
                    pq[:, :], AF.Identity, bias=bqT[:, fo:fo + 1]),
                    waits=(ev_mm, ev_bias_cols), incs=((s_act, 1),))
                ev_evq[j] = s_act.ev
        ev_prevq = (s_act.ev,)
    ph1.close()
    P.phase_barrier()
    if stop_after == "1a":
        return _stop()

    xfm_b = at("xfm_b", (128, DC * QGK), BF16, R2)
    wv_s = at("wv_s", (128, DC * KVD), BF16, R2 + DC * QGK * 2)
    xtm_b = at("xtm_b", (128, min(QGK // 128, 2) * D), BF16,
               R2 + DC * QGK * 2 + DC * KVD * 2)
    wk_s = at("wk_s", (128, DC * KVD), BF16, R3)
    repk = at("repk", (128, 2 * KVD), BF16, R3 + DC * KVD * 2)
    rotk_s = at("rotk_s", (128, KVD), BF16, R3 + DC * KVD * 2 + 4 * KVD)
    ph1b = contextlib.ExitStack()
    ptrA_b = ps("ptrA_b", (128, 1024), BF16, ph1b)
    ptrB_b = ps("ptrB_b", (128, 1024), BF16, ph1b)
    pk0 = ps("pk0", (128, QGK), F32, ph1b)
    pk1 = ps("pk1", (128, QGK), F32, ph1b)
    pv0 = ps("pv0", (128, KVD), F32, ph1b)
    pv1 = ps("pv1", (128, KVD), F32, ph1b)

    P.dma("sync", repk[:, 0:KVD], dr["cosk_rep"], s_repk)
    P.dma("sync", repk[:, KVD:], dr["sink_rep"], s_repk)
    ev_repk = (s_repk, s_repk.n)
    for fi in range(DC):
        P.dma("gpsimd", wk_s[:, fi * KVD:(fi + 1) * KVD],
              wslab_ap("wkT", fi, KVD), s_wk)
        P.dma("gpsimd", wv_s[:, fi * KVD:(fi + 1) * KVD],
              wslab_ap("wvT", fi, KVD), s_wk)
    ev_wkv_all = (s_wk, s_wk.n)
    ev_wk_rot = None
    for fi in range(DC):
        ev_wk_rot = rot_weight(wk_s[:, fi * KVD:(fi + 1) * KVD], repk,
                               rotk_s, KVD, KV, ev_wkv_all, ev_repk,
                               (ev_wk_rot,) if ev_wk_rot else None)

    ev_prevkv = ()
    ev_tp_k = ()
    for g in range(T_kv // QGK):
        ev_xfm, ev_ltp = load_transpose_group(xtm_b, xfm_b, ptrA_b, ptrB_b,
                                              g * QGK, QGK, QGK, ev_prevkv,
                                              ev_tp_k)
        ev_tp_k = (ev_ltp,)
        ev_evk = {0: None, 1: None}
        for fo in range(KVC):
            pk = pk0 if fo % 2 == 0 else pk1
            w = [ev_xfm, ev_wk_rot, ev_ones, ev_bias_cols]
            if ev_evk[fo % 2] is not None:
                w.append(ev_evk[fo % 2])
            for fi in range(DC):
                P.emit("tensor", lambda e, fi=fi, fo=fo, pk=pk: e.matmul(
                    pk[:, 0:QGK],
                    wk_s[:, fi * KVD + fo * 128:fi * KVD + (fo + 1) * 128],
                    xfm_b[:, fi * QGK:(fi + 1) * QGK],
                    start=(fi == 0), stop=(fi == DC - 1)),
                    waits=tuple(w) if fi == 0 else (),
                    incs=((s_pe, 1),) if fi == DC - 1 else ())
            ev_mm = s_pe.ev
            P.emit("scalar", lambda e, fo=fo, g=g, pk=pk: e.activation(
                k_P[:, fo * T_kv + g * QGK:fo * T_kv + (g + 1) * QGK],
                pk[:, 0:QGK], AF.Identity, bias=bqT[:, DC + fo:DC + fo + 1]),
                waits=(ev_mm,), incs=((s_act, 1),))
            ev_evk[fo % 2] = s_act.ev
        ev_evv = {0: None, 1: None}
        for tt in range(QGK // 128):
            pv = pv0 if tt % 2 == 0 else pv1
            w = [ev_xfm, ev_wkv_all, ev_ones, ev_brows]
            if ev_evv[tt % 2] is not None:
                w.append(ev_evv[tt % 2])
            for fi in range(DC):
                P.emit("tensor", lambda e, fi=fi, tt=tt, pv=pv: e.matmul(
                    pv[:, 0:KVD],
                    xfm_b[:, fi * QGK + tt * 128:fi * QGK + (tt + 1) * 128],
                    wv_s[:, fi * KVD:(fi + 1) * KVD],
                    start=(fi == 0), stop=False),
                    waits=tuple(w) if fi == 0 else ())
            P.emit("tensor", lambda e, pv=pv: e.matmul(
                pv[:, 0:KVD], ones_b[0:1, 0:128], bv_row[0:1, :],
                start=False, stop=True), incs=((s_pe, 1),))
            ev_mm = s_pe.ev
            st_i = (g * QGK) // 128 + tt
            P.emit("scalar", lambda e, st_i=st_i, pv=pv: e.copy(
                v_T[:, st_i * KVD:(st_i + 1) * KVD], pv[:, 0:KVD]),
                waits=(ev_mm,), incs=((s_act, 1),))
            ev_evv[tt % 2] = s_act.ev
        ev_prevkv = (s_act.ev,)
    ph1b.close()
    P.phase_barrier()
    if stop_after == "1b":
        return _stop()

    ev_qkv = s_act.ev
    if "qkv" in debug:
        dout("dbg_q", (128, H * T_own)); dout("dbg_k", (128, KVC * T_kv))
        dout("dbg_v", (128, nst * KVD))
        P.dma("gpsimd", dr["dbg_q"], q_P[:, :], s_dbg, waits=(ev_qkv,))
        P.dma("gpsimd", dr["dbg_k"], k_P[:, :], s_dbg, waits=(ev_qkv,))
        P.dma("gpsimd", dr["dbg_v"], v_T[:, :], s_dbg, waits=(ev_qkv,))

    nsq = at("nsq", (128, max(T_own, NRW)), BF16, R3)
    nrow = at("nrow", (1, max(T_own, NRW)), F32, R3 + max(T_own, NRW) * 2)
    ph1c = contextlib.ExitStack()
    kmaxv = sb("kmaxv", (1, 3 * KV), F32, ph1c)
    pnr = ps("pnr", (1, 512), F32, ph1c)
    ptb = ps("ptb", (128, nqt), F32, ph1c)

    nr_state = {"mm": (), "cp": ()}

    def norm_row(src_cols, n, ev_src):
        P.emit("scalar", lambda e, n=n: e.activation(nsq[:, 0:n], src_cols,
                                                     AF.Square),
               waits=tuple(ev_src) + nr_state["mm"], incs=((s_act, 1),))
        ev_sq = s_act.ev
        ev_pn = nr_state["cp"]
        for r in range((n + 511) // 512):
            w_ = min(512, n - r * 512)
            P.emit("tensor", lambda e, r=r, w_=w_: e.matmul(
                pnr[:, 0:w_], ones_c[:, :], nsq[:, r * 512:r * 512 + w_],
                start=True, stop=True),
                waits=(ev_sq, ev_ones) + ev_pn, incs=((s_pe, 1),))
            ev_mm = s_pe.ev
            P.emit("vector", lambda e, r=r, w_=w_: e.tensor_copy(
                nrow[0:1, r * 512:r * 512 + w_], pnr[0:1, 0:w_]),
                waits=(ev_mm,), incs=((s_dve, 1),))
            ev_pn = (s_dve.ev,)
        nr_state["mm"] = (s_pe.ev,)
        nr_state["cp"] = (s_dve.ev,)
        return s_dve.ev

    ev_kmax = None
    for kv in range(KV):
        nseg = 1 if "seg1" in ph1c_skip else (T_kv // NRW)
        for seg in range(nseg):
            ev_n = norm_row(k_P[:, kv * T_kv + seg * NRW:kv * T_kv + (seg + 1) * NRW],
                            NRW, (ev_qkv,))
            P.emit("vector", lambda e, kv=kv, seg=seg: e.reduce_max(
                kmaxv[0:1, 2 * KV + kv:2 * KV + kv + 1] if seg == 0 else
                kmaxv[0:1, KV + kv:KV + kv + 1],
                nrow[0:1, 0:NRW], AX.X),
                waits=(ev_n,), incs=((s_dve, 1),))
            if seg > 0 and "tmax" not in ph1c_skip:
                P.emit("vector", lambda e, kv=kv: e.tensor_max(
                    kmaxv[0:1, 2 * KV + kv:2 * KV + kv + 1],
                    kmaxv[0:1, 2 * KV + kv:2 * KV + kv + 1],
                    kmaxv[0:1, KV + kv:KV + kv + 1]),
                    waits=(s_dve.ev,), incs=((s_dve, 1),))
        ev_m = s_dve.ev
        P.emit("scalar", lambda e, kv=kv: e.activation(
            kmaxv[0:1, kv:kv + 1], kmaxv[0:1, 2 * KV + kv:2 * KV + kv + 1],
            AF.Ln), waits=(ev_m,), incs=((s_act, 1),))
        ev_l = s_act.ev
        P.emit("scalar", lambda e, kv=kv: e.activation(
            kmaxv[0:1, kv:kv + 1], kmaxv[0:1, kv:kv + 1], AF.Exp, scale=0.5),
            waits=(ev_l,), incs=((s_act, 1),))
        ev_kmax = s_act.ev

    for h in range(H):
        ev_n = norm_row(q_P[:, h * T_own:(h + 1) * T_own], T_own, (ev_qkv,))
        P.emit("scalar", lambda e: e.activation(
            nrow[0:1, 0:T_own], nrow[0:1, 0:T_own], AF.Ln),
            waits=(ev_n,), incs=((s_act, 1),))
        ev_l = s_act.ev
        P.emit("scalar", lambda e: e.activation(
            nrow[0:1, 0:T_own], nrow[0:1, 0:T_own], AF.Exp, scale=0.5),
            waits=(ev_l,), incs=((s_act, 1),))
        ev_s = s_act.ev
        kv = h // G
        P.emit("vector", lambda e, kv=kv: e.tensor_scalar(
            nrow[0:1, 0:T_own], nrow[0:1, 0:T_own],
            kmaxv[0:1, kv:kv + 1], -scale, ALU.mult, ALU.mult),
            waits=(ev_s, ev_kmax), incs=((s_dve, 1),))
        ev_b = s_dve.ev
        for qt in range(nqt):
            P.emit("tensor", lambda e, qt=qt: e.transpose(
                ptb[:, qt:qt + 1], nrow[0:1, qt * 128:(qt + 1) * 128],
                ident_f[0:1, 0:1]),
                waits=(ev_b, ev_consts) if qt == 0 else (),
                incs=((s_pe, 1),) if qt == nqt - 1 else ())
        ev_t = s_pe.ev
        P.emit("vector", lambda e, h=h: e.tensor_copy(
            bounds_c[:, h * nqt:(h + 1) * nqt], ptb[:, :]),
            waits=(ev_t,), incs=((s_dve, 1),))
    ev_bounds = s_dve.ev
    ph1c.close()
    P.phase_barrier()
    if stop_after == "1c":
        return _stop()

    pT = at("pT", (128, G * T_kv), BF16, R2)
    p_buf = at("p_buf", (128, 2 * T_kv), BF16, R3)
    ph2 = contextlib.ExitStack()
    QW = min(512, T_kv)
    nqr = T_kv // QW
    TPW = min(1024, T_kv)
    NPSC = 4
    pscs = [ps("psc%d" % i, (128, QW), F32, ph2) for i in range(NPSC)]
    ptp0 = ps("ptp0", (128, TPW), BF16, ph2)
    ptp1 = ps("ptp1", (128, TPW), BF16, ph2)
    ppv = ps("ppv", (128, G * 128), F32, ph2)
    dparts = sb("dparts", (128, 16 * G), F32, ph2)
    den = sb("den", (128, 2 * G), F32, ph2)
    biasv = sb("biasv", (128, 2 * G), F32, ph2)

    ev_psc_free = {i: () for i in range(NPSC)}
    ev_ptp_free = {0: (), 1: ()}
    ev_den_gh = [None] * G
    ev_exp_gh = [None] * G
    ev_pT_free = ()
    ev_pbuf_free = {0: (), 1: ()}
    ev_ppv_free = ()

    def ph2_head_meta(kvg, qt, gh):
        h = kvg * G + gh
        return {
            "kvg": kvg, "qt": qt, "gh": gh, "h": h,
            "dbase": 16 * gh, "bb": 2 * gh,
            "pbuf_i": (qt * G + gh) % 2,
            "c0": qt * 128,
            "bap": bounds_c[:, h * nqt + qt:h * nqt + qt + 1],
            "nexp": 0, "ev_exp": None,
        }

    def ph2_emit_bias(m):
        wb = (ev_bounds, ev_consts)
        if ev_exp_gh[m["gh"]] is not None:
            wb = wb + (ev_exp_gh[m["gh"]],)
        bb = m["bb"]
        P.emit("vector", lambda e, bap=m["bap"], bb=bb: e.tensor_scalar(
            biasv[:, bb:bb + 1], bap, 1.0, None, ALU.add),
            waits=wb, incs=((s_dve, 1),))
        P.emit("vector", lambda e, bap=m["bap"], bb=bb: e.tensor_tensor(
            biasv[:, bb + 1:bb + 2], bap, hflag[:, 0:1], ALU.add),
            incs=((s_dve, 1),))
        m["ev_bias"] = s_dve.ev

    def ph2_stage_S(m, qr0, qr1):
        h, qt, kvg, c0, bb = m["h"], m["qt"], m["kvg"], m["c0"], m["bb"]
        pb = p_buf[:, m["pbuf_i"] * T_kv:(m["pbuf_i"] + 1) * T_kv]
        for qr in range(qr0, qr1):
            psc = pscs[qr % NPSC]
            lo = qr * QW
            has_diag = lo <= c0 < lo + QW
            w0 = (ev_qkv, ev_consts) + ev_psc_free[qr % NPSC]
            P.emit("tensor", lambda e, h=h, qt=qt, lo=lo, kvg=kvg,
                   psc=psc: e.matmul(
                psc[:, 0:QW],
                q_P[:, h * T_own + qt * 128:h * T_own + (qt + 1) * 128],
                k_P[:, kvg * T_kv + lo:kvg * T_kv + lo + QW],
                start=True, stop=not has_diag),
                waits=w0, incs=() if has_diag else ((s_pe, 1),))
            if has_diag:
                P.emit("tensor", lambda e, psc=psc, o=c0 - lo: e.matmul(
                    psc[:, o:o + 128], ident_b[:, :], tril[:, :],
                    start=False, stop=True),
                    incs=((s_pe, 1),))
            ev_sc = s_pe.ev
            spans = []
            for (x0, x1, bi) in ((0, c0, 1), (c0, c0 + 128, 0),
                                 (c0 + 128, T_kv // 2, 0),
                                 (T_kv // 2, T_kv, 2)):
                r0_, r1_ = max(x0, lo), min(x1, lo + QW)
                if r0_ < r1_:
                    spans.append((r0_, r1_, bi))
            for (r0_, r1_, bi) in spans:
                ci = m["dbase"] + m["nexp"]
                m["nexp"] += 1
                bias_ap = m["bap"] if bi == 0 else biasv[:, bb + bi - 1:bb + bi]
                wexp = (ev_sc,)
                if m["nexp"] == 1:
                    wexp = wexp + (m["ev_bias"],) + ev_pbuf_free[m["pbuf_i"]]
                    if ev_den_gh[m["gh"]] is not None:
                        wexp = wexp + (ev_den_gh[m["gh"]],)
                P.emit("scalar", lambda e, r0_=r0_, r1_=r1_, psc=psc,
                       lo=lo, ci=ci, pb=pb, bias_ap=bias_ap: e.activation(
                    pb[:, r0_:r1_], psc[:, r0_ - lo:r1_ - lo], AF.Exp,
                    bias=bias_ap, scale=scale,
                    accum_out=dparts[:, ci:ci + 1]),
                    waits=wexp, incs=((s_act, 1),))
            ev_psc_free[qr % NPSC] = (s_act.ev,)
        if qr1 == nqr:
            m["ev_exp"] = s_act.ev
            ev_exp_gh[m["gh"]] = s_act.ev

    def ph2_stage_Tpre(m, ev_rd_gh):
        gh, dbase = m["gh"], m["dbase"]
        P.emit("vector", lambda e, nca=m["nexp"], gh=gh, dbase=dbase: e.reduce_sum(
            den[:, gh:gh + 1], dparts[:, dbase:dbase + nca], AX.X),
            waits=(m["ev_exp"],), incs=((s_dve, 1),))
        ev_den = s_dve.ev
        ev_den_gh[gh] = ev_den
        P.emit("vector", lambda e, gh=gh: e.reciprocal(
            den[:, G + gh:G + gh + 1], den[:, gh:gh + 1]),
            waits=(ev_den,), incs=((s_dve, 1),))
        ev_rd_gh[gh] = s_dve.ev

    def ph2_stage_Ttb(m, tb, ev_rd_gh, ev_pT_parts):
        gh = m["gh"]
        pb = p_buf[:, m["pbuf_i"] * T_kv:(m["pbuf_i"] + 1) * T_kv]
        ptp = ptp0 if tb % 2 == 0 else ptp1
        P.emit("vector", lambda e, pb=pb, gh=gh, tb=tb: e.tensor_scalar(
            pb[:, tb * TPW:(tb + 1) * TPW], pb[:, tb * TPW:(tb + 1) * TPW],
            den[:, G + gh:G + gh + 1], None, ALU.mult),
            waits=(ev_rd_gh[gh],), incs=((s_dve, 1),))
        ev_pn = s_dve.ev
        w1 = (ev_pn,) + ev_ptp_free[tb % 2] + ev_pT_free
        for sc in range(TPW // 128):
            scg = tb * (TPW // 128) + sc
            P.emit("tensor", lambda e, scg=scg, sc=sc, ptp=ptp,
                   pb=pb: e.transpose(
                ptp[:, sc * 128:(sc + 1) * 128],
                pb[:, scg * 128:(scg + 1) * 128],
                ident_b[:, :]),
                waits=w1 if sc == 0 else (),
                incs=((s_pe, 1),) if sc == TPW // 128 - 1 else ())
        ev_tp = s_pe.ev
        P.emit("vector", lambda e, gh=gh, tb=tb, ptp=ptp: e.tensor_copy(
            pT[:, gh * T_kv + tb * TPW:gh * T_kv + (tb + 1) * TPW],
            ptp[:, :]),
            waits=(ev_tp,), incs=((s_dve, 1),))
        ev_ptp_free[tb % 2] = (s_dve.ev,)
        ev_pT_parts.append(s_dve.ev)
        if tb == T_kv // TPW - 1:
            ev_pbuf_free[m["pbuf_i"]] = (s_dve.ev,)

    pairs = [(kvg, qt) for kvg in range(KV) for qt in range(nqt)]
    ntpb = T_kv // TPW
    QPT = max(1, (nqr + ntpb - 1) // ntpb)
    pend = None
    for pi, (kvg, qt) in enumerate(pairs):
        hs = [kvg * G + gh for gh in range(G)]
        ev_rd_gh = [None] * G
        ev_pT_parts = []
        if pend is None:
            pend = ph2_head_meta(kvg, qt, 0)
            ph2_emit_bias(pend)
            ph2_stage_S(pend, 0, nqr)
        for gh in range(G):
            cur = pend
            nxt = None
            if gh + 1 < G:
                nxt = ph2_head_meta(kvg, qt, gh + 1)
            elif pi + 1 < len(pairs):
                nxt = ph2_head_meta(*pairs[pi + 1], 0)
            if nxt is not None:
                ph2_emit_bias(nxt)
            ph2_stage_Tpre(cur, ev_rd_gh)
            for tb in range(ntpb):
                if nxt is not None:
                    ph2_stage_S(nxt, min(tb * QPT, nqr),
                                min((tb + 1) * QPT, nqr) if tb + 1 < ntpb else nqr)
                ph2_stage_Ttb(cur, tb, ev_rd_gh, ev_pT_parts)
            pend = nxt
        ev_pT_ready = (ev_pT_parts[-1],)
        w0 = ev_pT_ready + ev_ppv_free
        for sc in range(nst):
            for gh in range(G):
                P.emit("tensor", lambda e, sc=sc, gh=gh, kvg=kvg,
                       st=(sc == 0 and gh == 0),
                       sp=(sc == nst - 1 and gh == G - 1): e.matmul(
                    ppv[:, gh * 128:(gh + 1) * 128],
                    v_T[:, sc * KVD + kvg * 128:sc * KVD + (kvg + 1) * 128],
                    pT[:, gh * T_kv + sc * 128:gh * T_kv + (sc + 1) * 128],
                    start=st, stop=sp, skip_group_check=True),
                    waits=w0 if sc == 0 and gh == 0 else (),
                    incs=((s_pe, 1),) if sc == nst - 1 and gh == G - 1 else ())
        ev_pv = s_pe.ev
        ev_pT_free = (ev_pv,)
        for gh, h in enumerate(hs):
            P.emit("vector", lambda e, h=h, qt=qt, gh=gh: e.tensor_copy(
                q_P[:, h * T_own + qt * 128:h * T_own + (qt + 1) * 128],
                ppv[:, gh * 128:(gh + 1) * 128]),
                waits=(ev_pv,) if gh == 0 else (),
                incs=((s_dve, 1),) if gh == G - 1 else ())
        ev_ppv_free = (s_dve.ev,)
    ev_attnout = s_dve.ev
    ph2.close()
    P.phase_barrier()
    if stop_after == "2":
        return _stop()

    if "attn" in debug:
        dout("dbg_attn", (128, H * T_own))
        P.dma("gpsimd", dr["dbg_attn"], q_P[:, :], s_dbg, waits=(ev_attnout,))

    woT_s = at("woT_s", (128, DC * D), BF16, R1)
    xr = at("xr", (128, D), F32, R1 + DC * D * 2)
    x2_fm = at("x2_fm", (128, DC * TC), BF16, R2)
    rbuf = at("rbuf", (128, D), F32, R3)
    x2ff = at("x2ff", (128, D), F32, R3 + 4 * D)
    sqs = at("sqs", (128, D), BF16, R3 + 8 * D)
    ph3 = contextlib.ExitStack()
    stat = sb("stat", (128, 8), F32, ph3)
    lgf = sb("lgf", (EP1 if EP1 <= 128 else 128, min(512, T_own)), F32, ph3)
    py = ps("py", (128, D), F32, ph3)
    pgate = ps("pgate", (EP1, 512), F32, ph3)
    ptx2 = ps("ptx2", (128, min(1024, D)), F32, ph3)
    ptg = ps("ptg", (128, 4 * EP1), F32, ph3)
    GW = min(512, T_own)
    ngrp = GW // 128

    for fi in range(DC):
        P.dma("gpsimd", woT_s[:, fi * D:(fi + 1) * D], wslab_ap("woT", fi, D),
              s_wo)
    ev_wo = (s_wo, s_wo.n)

    ev_py_free = ()
    ev_xr_free = ()
    ev_r_free = ()
    ev_x2ff_free = ()
    ev_gate_ev = ()
    ev_ptx2_free = ()
    ev_ptg_free = ()
    ev_spill = []
    ev_x2fm_part = []
    rt = rbuf[:, 0:D]
    for tt in range(nqt):
        ev_xr = P.dma("sync", xr[:, :], dr["x_kv"][tt * 128:(tt + 1) * 128, :],
                      s_xr, waits=ev_xr_free)
        w0 = (ev_attnout, ev_wo, ev_ones, ev_brows) + ev_py_free
        for fi in range(DC):
            for fg in range(FOG):
                P.emit("tensor", lambda e, fi=fi, fg=fg, tt=tt: e.matmul(
                    py[:, fg * FW:(fg + 1) * FW],
                    q_P[:, fi * T_own + tt * 128:fi * T_own + (tt + 1) * 128],
                    woT_s[:, fi * D + fg * FW:fi * D + (fg + 1) * FW],
                    start=(fi == 0), stop=False,
                    skip_group_check=True),
                    waits=w0 if fi == 0 and fg == 0 else ())
        for fg in range(FOG):
            P.emit("tensor", lambda e, fg=fg: e.matmul(
                py[:, fg * FW:(fg + 1) * FW],
                ones_b[0:1, 0:128], bo_row[0:1, fg * FW:(fg + 1) * FW],
                start=False, stop=True, skip_group_check=True),
                incs=((s_pe, 1),) if fg == FOG - 1 else ())
        ev_y = s_pe.ev
        for fg in range(FOG):
            P.emit("vector", lambda e, fg=fg: e.tensor_add(
                rt[:, fg * FW:(fg + 1) * FW], py[:, fg * FW:(fg + 1) * FW],
                xr[:, fg * FW:(fg + 1) * FW]),
                waits=(ev_y, ev_xr) + ev_r_free if fg == 0 else (),
                incs=((s_dve, 1),) if fg == FOG - 1 else ())
        ev_r = s_dve.ev
        ev_py_free = (ev_r,)
        ev_xr_free = (ev_r,)
        P.emit("scalar", lambda e: e.activation(
            sqs[:, :], rt, AF.Square, accum_out=stat[:, 1:2]),
            waits=(ev_r,), incs=((s_act, 1),))
        ev_sq = s_act.ev
        P.emit("vector", lambda e: e.reduce_sum(stat[:, 0:1], rt, AX.X),
               waits=(ev_r,), incs=((s_dve, 1),))
        ev_sum = s_dve.ev
        P.emit("vector", lambda e: e.tensor_scalar(
            stat[:, 2:3], stat[:, 0:1], 1.0 / D, None, ALU.mult),
            waits=(ev_sum,), incs=((s_dve, 1),))
        ev_mean = s_dve.ev
        P.emit("vector", lambda e: e.tensor_scalar(
            stat[:, 3:4], stat[:, 1:2], 1.0 / D, None, ALU.mult),
            waits=(ev_sq,), incs=((s_dve, 1),))
        P.emit("vector", lambda e: e.tensor_tensor(
            stat[:, 4:5], stat[:, 2:3], stat[:, 2:3], ALU.mult),
            waits=(ev_mean,), incs=((s_dve, 1),))
        ev_m2 = s_dve.ev
        P.emit("vector", lambda e: e.tensor_tensor(
            stat[:, 5:6], stat[:, 3:4], stat[:, 4:5], ALU.subtract),
            waits=(ev_m2,), incs=((s_dve, 1),))
        ev_var = s_dve.ev
        P.emit("scalar", lambda e: e.activation(
            stat[:, 6:7], stat[:, 5:6], AF.Ln, bias=eps_c[:, 0:1]),
            waits=(ev_var,), incs=((s_act, 1),))
        ev_ln = s_act.ev
        P.emit("scalar", lambda e: e.activation(
            stat[:, 7:8], stat[:, 6:7], AF.Exp, scale=-0.5),
            waits=(ev_ln,), incs=((s_act, 1),))
        ev_rstd = s_act.ev
        P.emit("vector", lambda e: e.tensor_scalar(
            rt, rt, stat[:, 2:3], stat[:, 7:8], ALU.subtract, ALU.mult),
            waits=(ev_rstd,), incs=((s_dve, 1),))
        ev_x2 = s_dve.ev
        if "spill" not in ph3_skip:
            evsp = P.dma("sync", dr["x2sp"][tt * 128:(tt + 1) * 128, :], rt,
                         s_sp[tt % 2], waits=(ev_x2,))
            ev_spill.append(evsp)
        else:
            evsp = ev_x2
        if "tx" in ph3_skip:
            ev_r_free = (evsp,)
            continue
        np_ = min(1024, D) // 128
        ev_tx_evs = []
        ev_tx_last = None
        for phx in range((DC + np_ - 1) // np_):
            fis = list(range(phx * np_, min(DC, (phx + 1) * np_)))
            w1 = (ev_x2,) + ev_ptx2_free
            for k_, fi in enumerate(fis):
                P.emit("tensor", lambda e, fi=fi, k_=k_: e.transpose(
                    ptx2[:, k_ * 128:(k_ + 1) * 128],
                    rt[:, fi * 128:(fi + 1) * 128], ident_f[:, :]),
                    waits=w1 if k_ == 0 else (),
                    incs=((s_pe, 1),) if k_ == len(fis) - 1 else ())
            ev_tx = s_pe.ev
            ev_tx_last = ev_tx
            ev_ptx2_free = (ev_tx,)
            if "txv" not in ph3_skip:
                P.emit("vector", lambda e, phx=phx, np_=np_, n=len(fis): e.tensor_copy(
                    x2ff[:, phx * np_ * 128:phx * np_ * 128 + n * 128],
                    ptx2[:, 0:n * 128]),
                    waits=(ev_tx,) + ev_x2ff_free, incs=((s_dve, 1),))
                ev_tx_evs.append(s_dve.ev)
                ev_ptx2_free = (s_dve.ev,)
            if "txs" in ph3_skip:
                pass
            elif tt < ntt:
                ev_vc = ev_ptx2_free
                for k_, fi in enumerate(fis):
                    P.emit("scalar", lambda e, k_=k_, fi=fi, tt=tt: e.copy(
                        x2_fm[:, fi * TC + tt * 128:fi * TC + (tt + 1) * 128],
                        ptx2[:, k_ * 128:(k_ + 1) * 128]),
                        waits=((ev_tx,) + ev_vc) if k_ == 0 else (),
                        incs=((s_act, 1),) if k_ == len(fis) - 1 else ())
                ev_ptx2_free = ev_ptx2_free + (s_act.ev,)

        ev_x2ff = ev_tx_evs[-1] if ev_tx_evs else None
        if tt < ntt and "txs" not in ph3_skip:
            ev_x2fm_part.append(s_act.ev)
        ev_spf_tt = ()
        if tt >= ntt and "txv" not in ph3_skip:
            ch = tt // ntt
            off = (ch - 1) * DC * TC + (tt % ntt) * 128
            dst = bass.AP(dr["x2fmsp"].tensor, off,
                          [[(nch - 1) * DC * TC, 128], [TC, DC], [1, 128]])
            evspf = P.dma("gpsimd", dst, x2ff[:, 0:D], s_x2f,
                          waits=(ev_x2ff,))
            ev_spf_tt = (evspf,)
            ev_x2ff_free = ev_x2ff_free + ev_spf_tt
        ev_spf_all = (s_x2f, s_x2f.n)
        ev_r_free = (evsp, ev_tx_last)
        if "gate" in ph3_skip:
            ev_x2ff_free = (ev_x2ff,)
            continue
        gi = tt % ngrp
        for fi in range(DC):
            P.emit("tensor", lambda e, fi=fi, gi=gi,
                   st=(gi == 0 and fi == 0): e.matmul(
                pgate[:, gi * 128:(gi + 1) * 128],
                gseg_s[:, fi * EP1:(fi + 1) * EP1],
                x2ff[:, fi * 128:(fi + 1) * 128],
                start=st, stop=(fi == DC - 1), skip_group_check=True),
                waits=(ev_x2ff, ev_consts) + ev_gate_ev if fi == 0 else (),
                incs=((s_pe, 1),) if (fi == DC - 1) else ())
        ev_gmm = s_pe.ev
        ev_x2ff_free = (ev_gmm,) + ev_spf_tt
        if gi == ngrp - 1:
            P.emit("vector", lambda e: e.tensor_copy(lgf[:, :], pgate[0:EP1, 0:GW]),
                   waits=(ev_gmm,) + ev_ptg_free, incs=((s_dve, 1),))
            ev_lg = s_dve.ev
            ev_gate_ev = (ev_lg,)
            for g2 in range(ngrp):
                P.emit("tensor", lambda e, g2=g2: e.transpose(
                    ptg[:, g2 * EP1:(g2 + 1) * EP1],
                    lgf[0:EP1, g2 * 128:(g2 + 1) * 128],
                    ident_f[0:EP1, 0:EP1]),
                    waits=(ev_lg, ev_consts) + ev_ptg_free if g2 == 0 else (),
                    incs=((s_pe, 1),) if g2 == ngrp - 1 else ())
            ev_tg = s_pe.ev
            P.emit("vector", lambda e, tt=tt: e.tensor_copy(
                ltma[:, (tt - ngrp + 1) * EP1:(tt + 1) * EP1],
                ptg[:, 0:ngrp * EP1]),
                waits=(ev_tg,), incs=((s_dve, 1),))
            ev_ptg_free = (s_dve.ev,)
    ev_ln1_done = s_dve.ev
    ph3.close()
    P.phase_barrier()
    if stop_after == "3":
        return _stop()

    if "ln1" in debug:
        dout("dbg_x2", (T_own, D))
        for tt in range(nqt):
            P.dma("sync", dr["dbg_x2"][tt * 128:(tt + 1) * 128, :],
                  dr["x2sp"][tt * 128:(tt + 1) * 128, :], s_st[0],
                  waits=(s_sp[0].ev, s_sp[1].ev))
        dout("dbg_logits", (128, nqt * EP1))
        P.dma("sync", dr["dbg_logits"], ltma[:, :], s_st[0],
              waits=(ev_ln1_done,))

    acc = at("acc", (128, ntt * D), F32, R0)
    wring = at("wring", (128, NSLOT * SLOTW), BF16, R1)
    h_buf = at("h_buf", (128, max(FC, PFC) * TC), BF16, R3)
    sqs2 = at("sqs2", (128, D), BF16, R3)
    xr2 = at("xr2", (128, D), F32, R3 + 2 * D)
    sact = at("sact", (128, TC), BF16, R3 + max(FC, PFC) * TC * 2 + 2 * D)
    ph4 = contextlib.ExitStack()
    comb = sb("comb", (128, ntt * E), F32, ph4)
    mbuf = sb("mbuf", (128, Ktop * E), F32, ph4)
    curb = sb("curb", (128, E), F32, ph4)
    mvals = sb("mvals", (128, 8), F32, ph4)
    sigc = sb("sigc", (128, ntt), F32, ph4)
    stat2 = sb("stat2", (128, 8), F32, ph4)

    slot_free = [()] * NSLOT
    slot_i = [0]

    def load_unit(src_ap, width, waits=()):
        i = slot_i[0] % NSLOT
        slot_i[0] += 1
        sl = wring[:, i * SLOTW:i * SLOTW + width]
        ev = P.dma("gpsimd", sl, src_ap, s_w[i],
                   waits=tuple(waits) + slot_free[i])
        return i, sl, ev

    pg = ps("pg", (128, TC), F32, ph4)
    pu = ps("pu", (128, TC), F32, ph4)
    pdl = [ps("pd%d" % i, (128, FW), F32, ph4) for i in range(FOG)]
    st_ev_tt = {}
    for c in range(nch):
        t0 = c * TC
        ev_sp_all = (s_sp[0].ev, s_sp[1].ev)
        if c > 0:
            evld = P.dma("sync",
                         x2_fm[:, :],
                         dr["x2fmsp"][:, (c - 1) * DC * TC:c * DC * TC],
                         s_x2r, waits=((s_pe, s_pe.n), ev_spf_all))
            ev_x2fm = (evld,)
        else:
            ev_x2fm = tuple(ev_x2fm_part[-1:])

        ev_comb = []
        for tt in range(ntt):
            ttg = c * ntt + tt
            lsl = ltma[:, ttg * EP1:ttg * EP1 + E]
            P.emit("vector", lambda e, lsl=lsl: e.tensor_copy(curb[:, :], lsl),
                   waits=(ev_ln1_done,), incs=((s_dve, 1),))
            ev_cur = s_dve.ev
            ev_mask = None
            for j in range(Ktop):
                P.emit("vector", lambda e, j=j: e.reduce_max(
                    mvals[:, j:j + 1], curb[:, :], AX.X),
                    waits=(ev_cur,), incs=((s_dve, 1),))
                ev_mj = s_dve.ev
                P.emit("vector", lambda e, j=j: e.tensor_scalar(
                    mbuf[:, j * E:(j + 1) * E], curb[:, :], mvals[:, j:j + 1],
                    None, ALU.is_equal),
                    waits=(ev_mj,), incs=((s_dve, 1),))
                ev_mask = s_dve.ev
                if j < Ktop - 1:
                    P.emit("vector", lambda e, j=j: e.scalar_tensor_tensor(
                        curb[:, :], mbuf[:, j * E:(j + 1) * E], -1e30,
                        curb[:, :], ALU.mult, ALU.add),
                        waits=(ev_mask,), incs=((s_dve, 1),))
                    ev_cur = s_dve.ev
            P.emit("vector", lambda e: e.tensor_scalar(
                mvals[:, 4:5], mvals[:, 0:1], -1.0, None, ALU.mult),
                waits=(ev_mask,), incs=((s_dve, 1),))
            ev_neg = s_dve.ev
            P.emit("scalar", lambda e, K=Ktop: e.activation(
                mvals[:, 0:K], mvals[:, 0:K], AF.Exp, bias=mvals[:, 4:5],
                accum_out=mvals[:, 5:6]),
                waits=(ev_neg,), incs=((s_act, 1),))
            ev_ex = s_act.ev
            P.emit("vector", lambda e: e.reciprocal(mvals[:, 6:7], mvals[:, 5:6]),
                   waits=(ev_ex,), incs=((s_dve, 1),))
            ev_rc = s_dve.ev
            P.emit("vector", lambda e, K=Ktop: e.tensor_scalar(
                mvals[:, 0:K], mvals[:, 0:K], mvals[:, 6:7], None, ALU.mult),
                waits=(ev_rc,), incs=((s_dve, 1),))
            ev_cv = s_dve.ev
            for j in range(Ktop):
                if j == 0:
                    P.emit("vector", lambda e, tt=tt: e.tensor_scalar(
                        comb[:, tt * E:(tt + 1) * E], mbuf[:, 0:E],
                        mvals[:, 0:1], None, ALU.mult),
                        waits=(ev_cv,), incs=((s_dve, 1),))
                else:
                    P.emit("vector", lambda e, tt=tt, j=j: e.scalar_tensor_tensor(
                        comb[:, tt * E:(tt + 1) * E], mbuf[:, j * E:(j + 1) * E],
                        mvals[:, j:j + 1], comb[:, tt * E:(tt + 1) * E],
                        ALU.mult, ALU.add),
                        waits=(s_dve.ev,), incs=((s_dve, 1),))
            ev_comb.append(s_dve.ev)
            P.emit("scalar", lambda e, ttg=ttg: e.activation(
                mvals[:, 7:8], ltma[:, ttg * EP1 + E:ttg * EP1 + E + 1], AF.Exp,
                scale=-1.0),
                waits=(ev_ln1_done,), incs=((s_act, 1),))
            ev_sg = s_act.ev
            P.emit("vector", lambda e: e.tensor_scalar(
                mvals[:, 7:8], mvals[:, 7:8], 1.0, None, ALU.add),
                waits=(ev_sg,), incs=((s_dve, 1),))
            ev_sg1 = s_dve.ev
            P.emit("vector", lambda e, tt=tt: e.reciprocal(
                sigc[:, tt:tt + 1], mvals[:, 7:8]),
                waits=(ev_sg1,), incs=((s_dve, 1),))
        ev_gatep = s_dve.ev

        ev_pg_free = ()
        ev_pu_free = ()
        ev_pd_free = [()] * FOG
        ev_acc = {}

        def expert(kind, ei, first):
            FCur = PF if kind == "pse" else F
            FCc = PFC if kind == "pse" else FC
            nonlocal ev_pg_free, ev_pu_free
            hF = (FCc + 1) // 2

            def gu_ap(dname, hh):
                t = dr[dname].tensor
                base = ei * D * FCur + hh * hD * 128 * FCur
                return bass.AP(t, base, [[FCur, 128], [128 * FCur, hD], [1, FCur]])

            def d_ap(lo, hi):
                nsl = hi - lo
                if kind == "pse":
                    t = dr["sd"].tensor
                    base = (ei * PF + lo * 128) * D
                else:
                    t = dr["wd"].tensor
                    base = ei * F * D + lo * 128 * D
                return bass.AP(t, base, [[D, 128], [128 * D, nsl], [1, D]])

            gnm = "sgb" if kind == "pse" else "wg"
            unm = "sub" if kind == "pse" else "wu"
            sl_g = [load_unit(gu_ap(gnm, hh), hD * FCur) for hh in range(2)]
            sl_u = [load_unit(gu_ap(unm, hh), hD * FCur) for hh in range(2)]
            sl_d = []
            for hh in range(2):
                lo = hh * hF
                nsl = min(FCc, (hh + 1) * hF) - lo
                if nsl > 0:
                    sl_d.append((load_unit(d_ap(lo, lo + nsl), nsl * D), lo, nsl))

            def scol_ap(tt):
                if kind == "pse":
                    return sigc[:, tt:tt + 1]
                return comb[:, tt * E + ei:tt * E + ei + 1]

            def gu_mms(dst_psum, slots, ev_free, fo):
                for fi in range(DC):
                    ui = fi // hD
                    si, sl, evl = slots[ui]
                    col = (fi % hD) * FCur + fo * 128
                    w = ()
                    if fi == 0:
                        w = (evl,) + ev_x2fm + ev_free
                    elif fi % hD == 0:
                        w = (evl,)
                    for nsx in range(nfsub):
                        P.emit("tensor", lambda e, sl=sl, col=col, fi=fi,
                               nsx=nsx, dst=dst_psum: e.matmul(
                            dst[:, nsx * NF:(nsx + 1) * NF],
                            sl[:, col:col + 128],
                            x2_fm[:, fi * TC + nsx * NF:fi * TC + (nsx + 1) * NF],
                            start=(fi == 0), stop=(fi == DC - 1)),
                            waits=w if nsx == 0 else (),
                            incs=((s_pe, 1),)
                                 if fi == DC - 1 and nsx == nfsub - 1 else ())
                return s_pe.ev

            ev_hall = None
            for fo in range(FCc):
                ev_g = gu_mms(pg, sl_g, ev_pg_free, fo)
                if fo == FCc - 1:
                    for (si, sl, evl) in sl_g:
                        slot_free[si] = (ev_g,)
                P.emit("scalar", lambda e, pg=pg: e.activation(
                    sact[:, :], pg[:, :], AF.Silu),
                    waits=(ev_g,) + ev_pu_free, incs=((s_act, 1),))
                ev_si = s_act.ev
                ev_pg_free = (ev_si,)
                ev_u = gu_mms(pu, sl_u, ev_pu_free, fo)
                if fo == FCc - 1:
                    for (si, sl, evl) in sl_u:
                        slot_free[si] = (ev_u,)
                P.emit("vector", lambda e, fo=fo, pu=pu: e.tensor_tensor(
                    h_buf[:, fo * TC:(fo + 1) * TC], pu[:, :], sact[:, :],
                    ALU.mult),
                    waits=(ev_u, ev_si), incs=((s_dve, 1),))
                ev_pu_free = (s_dve.ev,)
                ev_hall = s_dve.ev
            for tt in range(ntt):
                ev_d_fg = [None] * FOG
                for fg in range(FOG):
                    for (slot, lo, nsl) in sl_d:
                        si, sl, evl = slot
                        for k_ in range(nsl):
                            f = lo + k_
                            w = ()
                            if fg == 0 and k_ == 0:
                                w = (evl, ev_hall)
                            if f == 0:
                                w = w + ev_pd_free[fg]
                            P.emit("tensor", lambda e, sl=sl, k_=k_, f=f, fg=fg,
                                   tt=tt, pd=pdl[fg]: e.matmul(
                                pd[:, 0:FW],
                                h_buf[:, f * TC + tt * 128:f * TC + (tt + 1) * 128],
                                sl[:, k_ * D + fg * FW:k_ * D + (fg + 1) * FW],
                                start=(f == 0), stop=(f == FCc - 1)),
                                waits=w,
                                incs=((s_pe, 1),) if f == FCc - 1 else ())
                    ev_d_fg[fg] = s_pe.ev
                    a = acc[:, tt * D + fg * FW:tt * D + (fg + 1) * FW]
                    if first:
                        P.emit("vector", lambda e, a=a, pd=pdl[fg],
                               sc_=scol_ap(tt): e.tensor_scalar(
                            a, pd[:, 0:FW], sc_, None, ALU.mult),
                            waits=(ev_d_fg[fg], ev_gatep) + st_ev_tt.get(tt, ()),
                            incs=((s_dve, 1),))
                    else:
                        P.emit("vector", lambda e, a=a, pd=pdl[fg],
                               sc_=scol_ap(tt): e.scalar_tensor_tensor(
                            a, pd[:, 0:FW], sc_, a, ALU.mult, ALU.add),
                            waits=(ev_d_fg[fg], ev_gatep), incs=((s_dve, 1),))
                    ev_pd_free[fg] = (s_dve.ev,)
                ev_d = s_pe.ev
                if tt == ntt - 1:
                    for (slot, lo, nsl) in sl_d:
                        slot_free[slot[0]] = (ev_d,)
                ev_acc[tt] = (s_dve.ev,)

        for ei in range(NPSE):
            expert("pse", ei, first=(ei == 0))
        for ei in range(E):
            expert("moe", ei, first=False)

        ev_pe_now = (s_pe, s_pe.n)
        ev_xr2_free = ()
        for tt in range(ntt):
            pi = tt % 2
            xt = xr2[:, :]
            ev_x2r = P.dma("sync", xt,
                           dr["x2sp"][t0 + tt * 128:t0 + (tt + 1) * 128, :],
                           s_x2l[pi],
                           waits=ev_sp_all + (ev_pe_now,) + ev_xr2_free)
            a = acc[:, tt * D:(tt + 1) * D]
            P.emit("vector", lambda e, a=a, xt=xt: e.tensor_add(a, a, xt),
                   waits=(ev_x2r,) + ev_acc.get(tt, ()), incs=((s_dve, 1),))
            ev_r2 = s_dve.ev
            ev_xr2_free = (ev_r2,)
            P.emit("scalar", lambda e, a=a: e.activation(
                sqs2[:, :], a, AF.Square, accum_out=stat2[:, 1:2]),
                waits=(ev_r2, ev_pe_now), incs=((s_act, 1),))
            ev_sq = s_act.ev
            P.emit("vector", lambda e, a=a: e.reduce_sum(stat2[:, 0:1], a, AX.X),
                   waits=(ev_r2,), incs=((s_dve, 1),))
            ev_sum = s_dve.ev
            P.emit("vector", lambda e: e.tensor_scalar(
                stat2[:, 2:3], stat2[:, 0:1], 1.0 / D, None, ALU.mult),
                waits=(ev_sum,), incs=((s_dve, 1),))
            ev_mean = s_dve.ev
            P.emit("vector", lambda e: e.tensor_scalar(
                stat2[:, 3:4], stat2[:, 1:2], 1.0 / D, None, ALU.mult),
                waits=(ev_sq,), incs=((s_dve, 1),))
            P.emit("vector", lambda e: e.tensor_tensor(
                stat2[:, 4:5], stat2[:, 2:3], stat2[:, 2:3], ALU.mult),
                waits=(ev_mean,), incs=((s_dve, 1),))
            ev_m2 = s_dve.ev
            P.emit("vector", lambda e: e.tensor_tensor(
                stat2[:, 5:6], stat2[:, 3:4], stat2[:, 4:5], ALU.subtract),
                waits=(ev_m2,), incs=((s_dve, 1),))
            ev_var = s_dve.ev
            P.emit("scalar", lambda e: e.activation(
                stat2[:, 6:7], stat2[:, 5:6], AF.Ln, bias=eps_c[:, 0:1]),
                waits=(ev_var,), incs=((s_act, 1),))
            ev_lnl = s_act.ev
            P.emit("scalar", lambda e: e.activation(
                stat2[:, 7:8], stat2[:, 6:7], AF.Exp, scale=-0.5),
                waits=(ev_lnl,), incs=((s_act, 1),))
            ev_rstd = s_act.ev
            P.emit("vector", lambda e, a=a: e.tensor_scalar(
                a, a, stat2[:, 2:3], stat2[:, 7:8], ALU.subtract, ALU.mult),
                waits=(ev_rstd, ev_pe_now), incs=((s_dve, 1),))
            ev_o = s_dve.ev
            st_ev_tt[tt] = (P.dma(
                "sync", dr["out"][t0 + tt * 128:t0 + (tt + 1) * 128, :], a,
                s_st[pi], waits=(ev_o,)),)
    ph4.close()
    P.emit("sync", lambda e: e.nop(),
           waits=((s_st[0], s_st[0].n), (s_st[1], s_st[1].n),
                  (s_dbg, s_dbg.n)))

    P.replay()
    es.close()
    return nc, dr


def kernel(**inputs):
    cfg = full_cfg()
    nc, dr = build_program(cfg)
    in_maps = [prep_core_inputs(inputs, c, cfg) for c in range(N_CORES)]
    res = run_bass_kernel_spmd(nc, in_maps, core_ids=list(range(N_CORES)))
    T_own, D = cfg["T_own"], cfg["D"]
    out = np.empty((cfg["B"], cfg["T_kv"], D), dtype=np.float32)
    for c in range(N_CORES):
        b, half = c // 2, c % 2
        out[b, half * T_own:(half + 1) * T_own, :] = res.results[c]["out"]
    return out


# version 11
# speedup vs baseline: 1.2158x; 1.2158x over previous
import math
import numpy as np
import ml_dtypes
import contextlib

import concourse.bass as bass
import concourse.mybir as mybir
from concourse.bass_utils import run_bass_kernel_spmd

F32 = mybir.dt.float32
BF16 = mybir.dt.bfloat16
AF = mybir.ActivationFunctionType
ALU = mybir.AluOpType
AX = mybir.AxisListType
BF = ml_dtypes.bfloat16

N_CORES = 8


def full_cfg():
    return dict(B=4, D=2048, H=16, KV=4, HD=128, T_own=2048, T_kv=4096,
                E=32, F=768, SF=5632, PF=512, TC=1024, K_top=4,
                eps=1e-5, theta=1e4)


def mini_cfg():
    return dict(B=2, D=256, H=2, KV=1, HD=128, T_own=256, T_kv=512,
                E=8, F=256, SF=512, PF=256, TC=128, K_top=4,
                eps=1e-5, theta=1e4)


def derived(cfg):
    d = dict(cfg)
    d["DC"] = cfg["D"] // 128
    d["KVD"] = cfg["KV"] * cfg["HD"]
    d["KVC"] = d["KVD"] // 128
    d["G"] = cfg["H"] // cfg["KV"]
    d["nqt"] = cfg["T_own"] // 128
    d["nst"] = cfg["T_kv"] // 128
    d["FC"] = cfg["F"] // 128
    d["PFC"] = cfg["PF"] // 128
    d["NPSE"] = cfg["SF"] // cfg["PF"]
    d["scale"] = cfg["HD"] ** -0.5
    d["QG"] = min(1024, cfg["T_own"])
    d["NSUB"] = min(512, d["QG"])
    d["nch"] = cfg["T_own"] // cfg["TC"]
    d["ntt"] = cfg["TC"] // 128
    d["NF"] = min(512, cfg["TC"])
    return d



def rope_tables(cfg):
    HD = cfg["HD"]
    inv = 1.0 / (cfg["theta"] ** (np.arange(0, HD, 2, dtype=np.float64) / HD))

    def flat(nh):
        ang = np.arange(nh, dtype=np.float64)[:, None] * inv[None, :]
        cos = np.cos(ang)
        sin = np.sin(ang)
        cosf = np.concatenate([cos, cos], axis=1).reshape(-1)
        sinf = np.concatenate([-sin, sin], axis=1).reshape(-1)
        return cosf.astype(np.float32), sinf.astype(np.float32)

    cq, sq = flat(cfg["H"])
    ck, sk = flat(cfg["KV"])
    return cq, sq, ck, sk


def host_constants(cfg):
    d = derived(cfg)
    cq, sq, ck, sk = rope_tables(cfg)
    return {
        "ident_bf": np.eye(128, dtype=BF),
        "ident_f32": np.eye(128, dtype=np.float32),
        "tril_os": np.tril(np.full((128, 128), 1.0 / d["scale"],
                                   dtype=np.float32)).astype(BF),
        "cosq_rep": np.broadcast_to(cq.astype(BF), (128, cfg["D"])).copy(),
        "sinq_rep": np.broadcast_to(sq.astype(BF), (128, cfg["D"])).copy(),
        "cosk_rep": np.broadcast_to(ck.astype(BF), (128, d["KVD"])).copy(),
        "sink_rep": np.broadcast_to(sk.astype(BF), (128, d["KVD"])).copy(),
        "cosq16": cq.reshape(cfg["H"], cfg["HD"]).copy(),
        "sinq16": sq.reshape(cfg["H"], cfg["HD"]).copy(),
        "cosk16": ck.reshape(cfg["KV"], cfg["HD"]).copy(),
        "sink16": sk.reshape(cfg["KV"], cfg["HD"]).copy(),
    }


def prep_core_inputs(inputs, core, cfg):
    d = derived(cfg)
    b = core // 2
    half = core % 2
    x = np.asarray(inputs["L0_attn_x"])
    x_kv = np.roll(x[b], -half * cfg["T_own"], axis=0)
    cm = {
        "x_kv": np.ascontiguousarray(x_kv, dtype=np.float32),
        "wqT": np.ascontiguousarray(np.asarray(inputs["L0_attn_wq"]).T),
        "wkT": np.ascontiguousarray(np.asarray(inputs["L0_attn_wk"]).T),
        "wvT": np.ascontiguousarray(np.asarray(inputs["L0_attn_wv"]).T),
        "woT": np.ascontiguousarray(np.asarray(inputs["L0_attn_wo"]).T),
        "bq": np.asarray(inputs["L0_attn_bq"]).reshape(1, -1).copy(),
        "bk": np.asarray(inputs["L0_attn_bk"]).reshape(1, -1).copy(),
        "bv": np.asarray(inputs["L0_attn_bv"]).reshape(1, -1).copy(),
        "bo": np.asarray(inputs["L0_attn_bo"]).reshape(1, -1).copy(),
        "gseg": np.ascontiguousarray(np.concatenate(
            [np.asarray(inputs["L0_ffn_gate_w"]),
             np.asarray(inputs["L0_ffn_seg"])], axis=1)),
        "wg": np.ascontiguousarray(inputs["L0_ffn_wg"]),
        "wu": np.ascontiguousarray(inputs["L0_ffn_wu"]),
        "wd": np.ascontiguousarray(inputs["L0_ffn_wd"]),
        "sgb": np.ascontiguousarray(
            np.asarray(inputs["L0_ffn_sg"]).reshape(cfg["D"], d["NPSE"], cfg["PF"])
            .transpose(1, 0, 2)),
        "sub": np.ascontiguousarray(
            np.asarray(inputs["L0_ffn_su"]).reshape(cfg["D"], d["NPSE"], cfg["PF"])
            .transpose(1, 0, 2)),
        "sd": np.ascontiguousarray(inputs["L0_ffn_sd"]),
        "hflag": np.full((128, 1), float(half), dtype=np.float32),
    }
    cm.update(host_constants(cfg))
    return cm



ENGS = ("sync", "scalar", "vector", "tensor", "gpsimd")


class S:
    __slots__ = ("h", "n", "name")

    def __init__(self, h, name):
        self.h = h
        self.n = 0
        self.name = name

    @property
    def ev(self):
        return (self, self.n)


class Plan:
    def __init__(self, nc, es):
        self.nc = nc
        self.items = {e: [] for e in ENGS}
        self.es = es
        self.all_sems = []

    def sem(self, name):
        s = S(self.es.enter_context(self.nc.semaphore(name)), name)
        self.all_sems.append(s)
        return s

    def phase_barrier(self):
        evs = tuple((s, s.n) for s in self.all_sems if s.n > 0)
        for eng in ENGS:
            self.emit(eng, lambda e: e.nop(), waits=evs)

    def emit(self, eng, fn, waits=(), incs=()):
        for s, v in waits:
            assert 0 <= v <= s.n, f"wait on future value of {s.name}: {v} > {s.n}"
        for s, k in incs:
            s.n += k
        self.items[eng].append((tuple(waits), fn, tuple(incs)))

    def dma(self, eng, out_ap, in_ap, sem, waits=()):
        self.emit(eng, lambda e, o=out_ap, i=in_ap: e.dma_start(o, i),
                  waits=waits, incs=((sem, 16),))
        return sem.ev

    def replay(self):
        nc = self.nc
        with nc.Block() as block:
            for engname in ENGS:
                items = self.items[engname]
                if not items:
                    continue

                def make(its):
                    def _(eng):
                        seen = {}
                        for waits, fn, incs in its:
                            for s, v in waits:
                                if v > 0 and seen.get(id(s), -1) < v:
                                    eng.wait_ge(s.h, v)
                                    seen[id(s)] = v
                            inst = fn(eng)
                            for s, k in incs:
                                inst.then_inc(s.h, k)
                    return _

                getattr(block, engname)(make(items))



def build_program(cfg, debug=(), BassCls=None, stop_after=None, ph3_skip=(), ph1c_skip=()):
    d = derived(cfg)
    DC, KVC, KVD = d["DC"], d["KVC"], d["KVD"]
    H, KV, G, HD = cfg["H"], cfg["KV"], d["G"], cfg["HD"]
    D, T_own, T_kv = cfg["D"], cfg["T_own"], cfg["T_kv"]
    E, F, FC = cfg["E"], cfg["F"], d["FC"]
    SF, PF, PFC, NPSE = cfg["SF"], cfg["PF"], d["PFC"], d["NPSE"]
    TC, ntt, NF, nch = cfg["TC"], d["ntt"], d["NF"], d["nch"]
    nqt, nst = d["nqt"], d["nst"]
    scale, eps = d["scale"], cfg["eps"]
    half = HD // 2
    QG = min(1024, T_own)
    NSUB = min(512, QG)
    nsub = QG // NSUB
    QGK = min(256, T_kv)
    nfsub = TC // NF
    FOG = max(1, D // 512)
    FW = min(512, D)
    hD = DC // 2
    EP1 = E + 1
    Ktop = cfg["K_top"]

    if BassCls is None:
        BassCls = bass.Bass
    nc = BassCls("TRN2", target_bir_lowering=False)

    dr = {}

    def din(name, shape, dt=F32):
        dr[name] = nc.dram_tensor(name, list(shape), dt, kind="ExternalInput").ap()

    def dout(name, shape, dt=F32):
        dr[name] = nc.dram_tensor(name, list(shape), dt, kind="ExternalOutput").ap()

    din("x_kv", (T_kv, D))
    din("wqT", (D, D)); din("wkT", (D, KVD)); din("wvT", (D, KVD)); din("woT", (D, D))
    din("bq", (1, D)); din("bk", (1, KVD)); din("bv", (1, KVD)); din("bo", (1, D))
    din("gseg", (D, E + 1))
    din("wg", (E, D, F)); din("wu", (E, D, F)); din("wd", (E, F, D))
    din("sgb", (NPSE, D, PF)); din("sub", (NPSE, D, PF)); din("sd", (SF, D))
    din("hflag", (128, 1))
    din("ident_bf", (128, 128), BF16); din("ident_f32", (128, 128))
    din("tril_os", (128, 128), BF16)
    din("cosq_rep", (128, D), BF16); din("sinq_rep", (128, D), BF16)
    din("cosk_rep", (128, KVD), BF16); din("sink_rep", (128, KVD), BF16)
    din("cosq16", (H, HD)); din("sinq16", (H, HD))
    din("cosk16", (KV, HD)); din("sink16", (KV, HD))
    dout("out", (T_own, D))
    dr["x2sp"] = nc.dram_tensor("x2sp", [T_own, D], F32, kind="Internal").ap()
    if nch > 1:
        dr["x2fmsp"] = nc.dram_tensor("x2fmsp", [128, (nch - 1) * DC * TC], BF16,
                                      kind="Internal").ap()

    es = contextlib.ExitStack()
    P = Plan(nc, es)
    s_const = P.sem("s_const")
    s_wq = P.sem("s_wq")
    s_wk = P.sem("s_wk")
    s_wo = P.sem("s_wo")
    s_repq = P.sem("s_repq")
    s_repk = P.sem("s_repk")
    s_x = P.sem("s_x")
    s_xr = P.sem("s_xr")
    s_sp = [P.sem("s_sp0"), P.sem("s_sp1")]
    s_x2r = P.sem("s_x2r")
    s_x2f = P.sem("s_x2f")
    s_x2l = [P.sem("s_x2l0"), P.sem("s_x2l1")]
    s_st = [P.sem("s_st0"), P.sem("s_st1")]
    s_dbg = P.sem("s_dbg")
    NSLOT = 6
    s_w = [P.sem("s_w%d" % i) for i in range(NSLOT)]
    s_pe = P.sem("s_pe")
    s_act = P.sem("s_act")
    s_dve = P.sem("s_dve")
    s_pl = P.sem("s_pl")

    SLOTW = max((DC // 2) * F, ((FC + 1) // 2) * D,
                (DC // 2) * PF, ((PFC + 1) // 2) * D)
    S0 = max(H * T_own * 2, ntt * D * 4)
    S1 = max(KVC * T_kv * 2 + nst * KVD * 2, DC * D * 2 + D * 4,
             NSLOT * SLOTW * 2, 4 * D * 4 + D * 2)
    S2 = max(DC * QG * 2, G * T_kv * 2, DC * TC * 2,
             DC * QGK * 2 + DC * KVD * 2 + (QGK // 128) * D * 2)
    NRW = min(2048, T_kv)
    S3 = max(10 * D,
             DC * KVD * 2 + 6 * KVD,
             6 * max(T_own, NRW),
             4 * T_kv,
             max(FC, PFC) * TC * 2 + 2 * D + 2 * TC)
    RTOP = nc._sbuf_addr_for_side("right")
    AB = RTOP - (S0 + S1 + S2 + S3)
    R0, R1, R2, R3 = AB, AB + S0, AB + S0 + S1, AB + S0 + S1 + S2

    def at(name, shape, dt, off):
        return nc.alloc_sbuf_tensor_at(name, list(shape), dt, offset=off)

    def sb(name, shape, dt, sc=es):
        return sc.enter_context(nc.sbuf_tensor(name, list(shape), dt))

    def ps(name, shape, dt, sc=es):
        return sc.enter_context(nc.psum_tensor(name, list(shape), dt))

    q_P = at("q_P", (128, H * T_own), BF16, R0)
    k_P = at("k_P", (128, KVC * T_kv), BF16, R1)
    v_T = at("v_T", (128, nst * KVD), BF16, R1 + KVC * T_kv * 2)
    ident_b = sb("ident_b", (128, 128), BF16)
    ident_f = sb("ident_f", (128, 128), F32)
    tril = sb("tril", (128, 128), BF16)
    hflag = sb("hflag_s", (128, 1), F32)
    ones_b = sb("ones_b", (1, 512), BF16)
    ones_c = sb("ones_c", (128, 1), BF16)
    eps_c = sb("eps_c", (128, 1), F32)
    gseg_s = sb("gseg_s", (128, DC * EP1), F32)
    ltma = sb("ltma", (128, nqt * EP1), F32)
    bounds_c = sb("bounds_c", (128, H * nqt), F32)
    cs16 = sb("cs16", (max(H, 2), 4 * HD), F32)
    b16 = sb("b16", (max(H, 2), 4 * HD), F32)
    bqT = sb("bqT", (128, DC + KVC), F32)
    bv_row = sb("bv_row", (1, KVD), BF16)
    bo_row = sb("bo_row", (1, D), BF16)

    for nm, t in (("ident_bf", ident_b), ("ident_f32", ident_f),
                  ("tril_os", tril), ("hflag", hflag)):
        P.dma("sync", t[:, :], dr[nm], s_const)
    for fi in range(DC):
        P.dma("sync", gseg_s[:, fi * EP1:(fi + 1) * EP1],
              dr["gseg"][fi * 128:(fi + 1) * 128, :], s_const)
    P.dma("sync", cs16[0:H, 0:HD], dr["cosq16"], s_const)
    P.dma("sync", cs16[0:H, HD:2 * HD], dr["sinq16"], s_const)
    P.dma("sync", cs16[0:KV, 2 * HD:3 * HD], dr["cosk16"], s_const)
    P.dma("sync", cs16[0:KV, 3 * HD:4 * HD], dr["sink16"], s_const)
    P.dma("gpsimd", bv_row[:, :], dr["bv"], s_dbg)
    P.dma("gpsimd", bo_row[:, :], dr["bo"], s_dbg)
    ev_brows = (s_dbg, s_dbg.n)
    P.emit("vector", lambda e: e.memset(ones_b[:, :], 1.0))
    P.emit("vector", lambda e: e.memset(ones_c[:, :], 1.0))
    P.emit("vector", lambda e: e.memset(eps_c[:, :], float(eps)),
           incs=((s_dve, 1),))
    ev_ones = s_dve.ev
    P.dma("sync", b16[0:DC, 0:HD],
          bass.AP(dr["bq"].tensor, 0, [[128, DC], [1, 128]]), s_const)
    P.dma("sync", b16[0:KVC, HD:2 * HD],
          bass.AP(dr["bk"].tensor, 0, [[128, KVC], [1, 128]]), s_const)
    ev_consts = (s_const, s_const.n)

    pbt_scope = contextlib.ExitStack()
    pbt = ps("pbt", (128, DC + KVC), F32, pbt_scope)

    def bias_cols(src_c0, C, coff, soff, dst_off, ev_prev):
        s_ = b16[0:C, src_c0:src_c0 + HD]
        P.emit("vector", lambda e: e.tensor_mul(
            b16[0:C, 2 * HD:2 * HD + half], s_[0:C, half:HD],
            cs16[0:C, soff:soff + half]),
            waits=(ev_consts,) + ev_prev, incs=((s_dve, 1),))
        P.emit("vector", lambda e: e.tensor_mul(
            b16[0:C, 2 * HD + half:3 * HD], s_[0:C, 0:half],
            cs16[0:C, soff + half:soff + HD]),
            incs=((s_dve, 1),))
        P.emit("vector", lambda e: e.tensor_mul(
            b16[0:C, 3 * HD:4 * HD], s_, cs16[0:C, coff:coff + HD]),
            incs=((s_dve, 1),))
        ev_m3 = s_dve.ev
        P.emit("vector", lambda e: e.tensor_add(
            s_, b16[0:C, 2 * HD:3 * HD], b16[0:C, 3 * HD:4 * HD]),
            waits=(ev_m3,), incs=((s_dve, 1),))
        ev_r = s_dve.ev
        P.emit("tensor", lambda e, C=C: e.transpose(
            pbt[:, dst_off:dst_off + C], s_, ident_f[0:C, 0:C]),
            waits=(ev_r, ev_consts), incs=((s_pe, 1),))
        ev_t = s_pe.ev
        P.emit("vector", lambda e, C=C: e.tensor_copy(
            bqT[:, dst_off:dst_off + C], pbt[:, dst_off:dst_off + C]),
            waits=(ev_t,), incs=((s_dve, 1),))
        return s_dve.ev

    def _stop():
        P.replay()
        es.close()
        return nc, dr

    ev_bq = bias_cols(0, DC, 0, HD, 0, ())
    ev_bias_cols = bias_cols(HD, KVC, 2 * HD, 3 * HD, DC, (ev_bq,))
    pbt_scope.close()
    P.phase_barrier()
    if stop_after == "consts":
        return _stop()

    def x_rows_ap(r0, ntiles):
        return bass.AP(dr["x_kv"].tensor, r0 * D,
                       [[D, 128], [128 * D, ntiles], [1, D]])

    def wslab_ap(dname, fi, width):
        return dr[dname][fi * 128:(fi + 1) * 128, :]

    def rot_weight(ws, rep, rscr, width, nheads, ev_load, ev_rep, ev_prev):
        w0 = (ev_load, ev_rep) + (ev_prev if ev_prev else ())
        for h in range(nheads):
            P.emit("vector", lambda e, h=h: e.tensor_mul(
                rscr[:, h * HD:h * HD + half],
                ws[:, h * HD + half:h * HD + HD],
                rep[:, width + h * HD:width + h * HD + half]),
                waits=w0 if h == 0 else ())
            P.emit("vector", lambda e, h=h: e.tensor_mul(
                rscr[:, h * HD + half:h * HD + HD],
                ws[:, h * HD:h * HD + half],
                rep[:, width + h * HD + half:width + h * HD + HD]),
                incs=((s_dve, 1),) if h == nheads - 1 else ())
        ev_sin = s_dve.ev
        P.emit("vector", lambda e: e.tensor_mul(ws, ws, rep[:, 0:width]),
               waits=(ev_sin,), incs=((s_dve, 1),))
        evc = s_dve.ev
        P.emit("vector", lambda e: e.tensor_add(ws, ws, rscr[:, 0:width]),
               waits=(evc,), incs=((s_dve, 1),))
        return s_dve.ev

    wq_s = at("wq_s", (128, DC * D), BF16, R1)
    xfm = at("xfm", (128, DC * QG), BF16, R2)
    xtm = at("xtm", (128, min(QG // 128, 2) * D), BF16, R3)
    repq = at("repq", (128, 2 * D), BF16, R3 + min(QG // 128, 2) * D * 2)
    rot_s = at("rot_s", (128, D), BF16, R3 + min(QG // 128, 2) * D * 2 + 4 * D)
    ph1 = contextlib.ExitStack()
    ptrA = ps("ptrA", (128, 1024), BF16, ph1)
    ptrB = ps("ptrB", (128, 1024), BF16, ph1)
    pq0 = ps("pq0", (128, QG), F32, ph1)
    pq1 = ps("pq1", (128, QG), F32, ph1)

    P.dma("sync", repq[:, 0:D], dr["cosq_rep"], s_repq)
    P.dma("sync", repq[:, D:2 * D], dr["sinq_rep"], s_repq)
    ev_repq = (s_repq, s_repq.n)
    for fi in range(DC):
        P.dma("gpsimd", wq_s[:, fi * D:(fi + 1) * D], wslab_ap("wqT", fi, D), s_wq)
    ev_wq_all = (s_wq, s_wq.n)
    ev_wq_rot = None
    for fi in range(DC):
        ev_wq_rot = rot_weight(wq_s[:, fi * D:(fi + 1) * D], repq, rot_s,
                               D, H, ev_wq_all, ev_repq,
                               (ev_wq_rot,) if ev_wq_rot else None)

    def load_transpose_group(xtm_t, xfm_t, ptA, ptB, r0, ntok, width_cols,
                             ev_prev_mms, ev_dma_first=()):
        nhalf = xtm_t.shape[1] // D
        ev_ev = {0: None, 1: None}
        ev_dma_prev = tuple(ev_dma_first)
        ev_last_tp = None
        for hf in range(ntok // (nhalf * 128)):
            hr0 = r0 + hf * nhalf * 128
            evd = P.dma("gpsimd", xtm_t[:, :], x_rows_ap(hr0, nhalf), s_x,
                        waits=ev_dma_prev)
            for fi in range(DC):
                pp = fi % 2
                ptr = ptA if pp == 0 else ptB
                w0 = [evd, ev_consts]
                if ev_ev[pp] is not None:
                    w0.append(ev_ev[pp])
                if fi == 0 and ev_prev_mms:
                    w0.extend(ev_prev_mms)
                for tt in range(nhalf):
                    P.emit("tensor", lambda e, fi=fi, tt=tt, ptr=ptr, hf=hf,
                           xt=xtm_t: e.transpose(
                        ptr[:, tt * 128:(tt + 1) * 128],
                        xt[:, tt * D + fi * 128:tt * D + (fi + 1) * 128],
                        ident_b[:, :]),
                        waits=tuple(w0) if tt == 0 else (),
                        incs=((s_pe, 1),) if tt == nhalf - 1 else ())
                ev_t = s_pe.ev
                ev_last_tp = ev_t
                P.emit("vector", lambda e, fi=fi, ptr=ptr, hf=hf, nh=nhalf,
                       xf=xfm_t: e.tensor_copy(
                    xf[:, fi * width_cols + hf * nh * 128:
                       fi * width_cols + (hf + 1) * nh * 128],
                    ptr[:, 0:nh * 128]),
                    waits=(ev_t,), incs=((s_dve, 1),))
                ev_ev[pp] = s_dve.ev
            ev_dma_prev = (ev_last_tp,)
        return s_dve.ev, ev_last_tp

    ev_prevq = ()
    ev_tp_q = ()
    for g in range(T_own // QG):
        ev_xfm, ev_ltp = load_transpose_group(xtm, xfm, ptrA, ptrB, g * QG, QG,
                                              QG, ev_prevq, ev_tp_q)
        ev_tp_q = (ev_ltp,)
        ev_evq = {0: None, 1: None}
        for pr in range(DC // 2):
            for j, pq in ((0, pq0), (1, pq1)):
                fo = 2 * pr + j
                w = [ev_xfm, ev_wq_rot, ev_ones]
                if ev_evq[j] is not None:
                    w.append(ev_evq[j])
                for fi in range(DC):
                    for ns in range(nsub):
                        stq = (fi == 0) and (ns == 0 or (ns * NSUB * 4) % 2048 == 0)
                        P.emit("tensor", lambda e, fi=fi, fo=fo, ns=ns, pq=pq,
                               stq=stq: e.matmul(
                            pq[:, ns * NSUB:(ns + 1) * NSUB],
                            wq_s[:, fi * D + fo * 128:fi * D + (fo + 1) * 128],
                            xfm[:, fi * QG + ns * NSUB:fi * QG + (ns + 1) * NSUB],
                            start=stq, stop=(fi == DC - 1),
                            skip_group_check=True),
                            waits=tuple(w) if fi == 0 and ns == 0 else (),
                            incs=((s_pe, 1),)
                                 if fi == DC - 1 and ns == nsub - 1 else ())
                ev_mm = s_pe.ev
                P.emit("scalar", lambda e, fo=fo, g=g, pq=pq: e.activation(
                    q_P[:, fo * T_own + g * QG:fo * T_own + (g + 1) * QG],
                    pq[:, :], AF.Identity, bias=bqT[:, fo:fo + 1]),
                    waits=(ev_mm, ev_bias_cols), incs=((s_act, 1),))
                ev_evq[j] = s_act.ev
        ev_prevq = (s_act.ev,)
    ph1.close()
    P.phase_barrier()
    if stop_after == "1a":
        return _stop()

    xfm_b = at("xfm_b", (128, DC * QGK), BF16, R2)
    wv_s = at("wv_s", (128, DC * KVD), BF16, R2 + DC * QGK * 2)
    xtm_b = at("xtm_b", (128, min(QGK // 128, 2) * D), BF16,
               R2 + DC * QGK * 2 + DC * KVD * 2)
    wk_s = at("wk_s", (128, DC * KVD), BF16, R3)
    repk = at("repk", (128, 2 * KVD), BF16, R3 + DC * KVD * 2)
    rotk_s = at("rotk_s", (128, KVD), BF16, R3 + DC * KVD * 2 + 4 * KVD)
    ph1b = contextlib.ExitStack()
    ptrA_b = ps("ptrA_b", (128, 1024), BF16, ph1b)
    ptrB_b = ps("ptrB_b", (128, 1024), BF16, ph1b)
    pk0 = ps("pk0", (128, QGK), F32, ph1b)
    pk1 = ps("pk1", (128, QGK), F32, ph1b)
    pv0 = ps("pv0", (128, KVD), F32, ph1b)
    pv1 = ps("pv1", (128, KVD), F32, ph1b)

    P.dma("sync", repk[:, 0:KVD], dr["cosk_rep"], s_repk)
    P.dma("sync", repk[:, KVD:], dr["sink_rep"], s_repk)
    ev_repk = (s_repk, s_repk.n)
    for fi in range(DC):
        P.dma("gpsimd", wk_s[:, fi * KVD:(fi + 1) * KVD],
              wslab_ap("wkT", fi, KVD), s_wk)
        P.dma("gpsimd", wv_s[:, fi * KVD:(fi + 1) * KVD],
              wslab_ap("wvT", fi, KVD), s_wk)
    ev_wkv_all = (s_wk, s_wk.n)
    ev_wk_rot = None
    for fi in range(DC):
        ev_wk_rot = rot_weight(wk_s[:, fi * KVD:(fi + 1) * KVD], repk,
                               rotk_s, KVD, KV, ev_wkv_all, ev_repk,
                               (ev_wk_rot,) if ev_wk_rot else None)

    ev_prevkv = ()
    ev_tp_k = ()
    for g in range(T_kv // QGK):
        ev_xfm, ev_ltp = load_transpose_group(xtm_b, xfm_b, ptrA_b, ptrB_b,
                                              g * QGK, QGK, QGK, ev_prevkv,
                                              ev_tp_k)
        ev_tp_k = (ev_ltp,)
        ev_evk = {0: None, 1: None}
        for fo in range(KVC):
            pk = pk0 if fo % 2 == 0 else pk1
            w = [ev_xfm, ev_wk_rot, ev_ones, ev_bias_cols]
            if ev_evk[fo % 2] is not None:
                w.append(ev_evk[fo % 2])
            for fi in range(DC):
                P.emit("tensor", lambda e, fi=fi, fo=fo, pk=pk: e.matmul(
                    pk[:, 0:QGK],
                    wk_s[:, fi * KVD + fo * 128:fi * KVD + (fo + 1) * 128],
                    xfm_b[:, fi * QGK:(fi + 1) * QGK],
                    start=(fi == 0), stop=(fi == DC - 1)),
                    waits=tuple(w) if fi == 0 else (),
                    incs=((s_pe, 1),) if fi == DC - 1 else ())
            ev_mm = s_pe.ev
            P.emit("scalar", lambda e, fo=fo, g=g, pk=pk: e.activation(
                k_P[:, fo * T_kv + g * QGK:fo * T_kv + (g + 1) * QGK],
                pk[:, 0:QGK], AF.Identity, bias=bqT[:, DC + fo:DC + fo + 1]),
                waits=(ev_mm,), incs=((s_act, 1),))
            ev_evk[fo % 2] = s_act.ev
        ev_evv = {0: None, 1: None}
        for tt in range(QGK // 128):
            pv = pv0 if tt % 2 == 0 else pv1
            w = [ev_xfm, ev_wkv_all, ev_ones, ev_brows]
            if ev_evv[tt % 2] is not None:
                w.append(ev_evv[tt % 2])
            for fi in range(DC):
                P.emit("tensor", lambda e, fi=fi, tt=tt, pv=pv: e.matmul(
                    pv[:, 0:KVD],
                    xfm_b[:, fi * QGK + tt * 128:fi * QGK + (tt + 1) * 128],
                    wv_s[:, fi * KVD:(fi + 1) * KVD],
                    start=(fi == 0), stop=False),
                    waits=tuple(w) if fi == 0 else ())
            P.emit("tensor", lambda e, pv=pv: e.matmul(
                pv[:, 0:KVD], ones_b[0:1, 0:128], bv_row[0:1, :],
                start=False, stop=True), incs=((s_pe, 1),))
            ev_mm = s_pe.ev
            st_i = (g * QGK) // 128 + tt
            P.emit("scalar", lambda e, st_i=st_i, pv=pv: e.copy(
                v_T[:, st_i * KVD:(st_i + 1) * KVD], pv[:, 0:KVD]),
                waits=(ev_mm,), incs=((s_act, 1),))
            ev_evv[tt % 2] = s_act.ev
        ev_prevkv = (s_act.ev,)
    ph1b.close()
    P.phase_barrier()
    if stop_after == "1b":
        return _stop()

    ev_qkv = s_act.ev
    if "qkv" in debug:
        dout("dbg_q", (128, H * T_own)); dout("dbg_k", (128, KVC * T_kv))
        dout("dbg_v", (128, nst * KVD))
        P.dma("gpsimd", dr["dbg_q"], q_P[:, :], s_dbg, waits=(ev_qkv,))
        P.dma("gpsimd", dr["dbg_k"], k_P[:, :], s_dbg, waits=(ev_qkv,))
        P.dma("gpsimd", dr["dbg_v"], v_T[:, :], s_dbg, waits=(ev_qkv,))

    nsq = at("nsq", (128, max(T_own, NRW)), BF16, R3)
    nrow = at("nrow", (1, max(T_own, NRW)), F32, R3 + max(T_own, NRW) * 2)
    ph1c = contextlib.ExitStack()
    kmaxv = sb("kmaxv", (1, 3 * KV), F32, ph1c)
    pnr = ps("pnr", (1, 512), F32, ph1c)
    ptb = ps("ptb", (128, nqt), F32, ph1c)

    nr_state = {"mm": (), "cp": ()}

    def norm_row(src_cols, n, ev_src):
        P.emit("scalar", lambda e, n=n: e.activation(nsq[:, 0:n], src_cols,
                                                     AF.Square),
               waits=tuple(ev_src) + nr_state["mm"], incs=((s_act, 1),))
        ev_sq = s_act.ev
        ev_pn = nr_state["cp"]
        for r in range((n + 511) // 512):
            w_ = min(512, n - r * 512)
            P.emit("tensor", lambda e, r=r, w_=w_: e.matmul(
                pnr[:, 0:w_], ones_c[:, :], nsq[:, r * 512:r * 512 + w_],
                start=True, stop=True),
                waits=(ev_sq, ev_ones) + ev_pn, incs=((s_pe, 1),))
            ev_mm = s_pe.ev
            P.emit("vector", lambda e, r=r, w_=w_: e.tensor_copy(
                nrow[0:1, r * 512:r * 512 + w_], pnr[0:1, 0:w_]),
                waits=(ev_mm,), incs=((s_dve, 1),))
            ev_pn = (s_dve.ev,)
        nr_state["mm"] = (s_pe.ev,)
        nr_state["cp"] = (s_dve.ev,)
        return s_dve.ev

    ev_kmax = None
    for kv in range(KV):
        nseg = 1 if "seg1" in ph1c_skip else (T_kv // NRW)
        for seg in range(nseg):
            ev_n = norm_row(k_P[:, kv * T_kv + seg * NRW:kv * T_kv + (seg + 1) * NRW],
                            NRW, (ev_qkv,))
            P.emit("vector", lambda e, kv=kv, seg=seg: e.reduce_max(
                kmaxv[0:1, 2 * KV + kv:2 * KV + kv + 1] if seg == 0 else
                kmaxv[0:1, KV + kv:KV + kv + 1],
                nrow[0:1, 0:NRW], AX.X),
                waits=(ev_n,), incs=((s_dve, 1),))
            if seg > 0 and "tmax" not in ph1c_skip:
                P.emit("vector", lambda e, kv=kv: e.tensor_max(
                    kmaxv[0:1, 2 * KV + kv:2 * KV + kv + 1],
                    kmaxv[0:1, 2 * KV + kv:2 * KV + kv + 1],
                    kmaxv[0:1, KV + kv:KV + kv + 1]),
                    waits=(s_dve.ev,), incs=((s_dve, 1),))
        ev_m = s_dve.ev
        P.emit("scalar", lambda e, kv=kv: e.activation(
            kmaxv[0:1, kv:kv + 1], kmaxv[0:1, 2 * KV + kv:2 * KV + kv + 1],
            AF.Ln), waits=(ev_m,), incs=((s_act, 1),))
        ev_l = s_act.ev
        P.emit("scalar", lambda e, kv=kv: e.activation(
            kmaxv[0:1, kv:kv + 1], kmaxv[0:1, kv:kv + 1], AF.Exp, scale=0.5),
            waits=(ev_l,), incs=((s_act, 1),))
        ev_kmax = s_act.ev

    for h in range(H):
        ev_n = norm_row(q_P[:, h * T_own:(h + 1) * T_own], T_own, (ev_qkv,))
        P.emit("scalar", lambda e: e.activation(
            nrow[0:1, 0:T_own], nrow[0:1, 0:T_own], AF.Ln),
            waits=(ev_n,), incs=((s_act, 1),))
        ev_l = s_act.ev
        P.emit("scalar", lambda e: e.activation(
            nrow[0:1, 0:T_own], nrow[0:1, 0:T_own], AF.Exp, scale=0.5),
            waits=(ev_l,), incs=((s_act, 1),))
        ev_s = s_act.ev
        kv = h // G
        P.emit("vector", lambda e, kv=kv: e.tensor_scalar(
            nrow[0:1, 0:T_own], nrow[0:1, 0:T_own],
            kmaxv[0:1, kv:kv + 1], -scale, ALU.mult, ALU.mult),
            waits=(ev_s, ev_kmax), incs=((s_dve, 1),))
        ev_b = s_dve.ev
        for qt in range(nqt):
            P.emit("tensor", lambda e, qt=qt: e.transpose(
                ptb[:, qt:qt + 1], nrow[0:1, qt * 128:(qt + 1) * 128],
                ident_f[0:1, 0:1]),
                waits=(ev_b, ev_consts) if qt == 0 else (),
                incs=((s_pe, 1),) if qt == nqt - 1 else ())
        ev_t = s_pe.ev
        P.emit("vector", lambda e, h=h: e.tensor_copy(
            bounds_c[:, h * nqt:(h + 1) * nqt], ptb[:, :]),
            waits=(ev_t,), incs=((s_dve, 1),))
    ev_bounds = s_dve.ev
    ph1c.close()
    P.phase_barrier()
    if stop_after == "1c":
        return _stop()

    pT = at("pT", (128, G * T_kv), BF16, R2)
    p_buf = at("p_buf", (128, 2 * T_kv), BF16, R3)
    ph2 = contextlib.ExitStack()
    QW = min(512, T_kv)
    nqr = T_kv // QW
    TPW = min(1024, T_kv)
    NPSC = 4
    pscs = [ps("psc%d" % i, (128, QW), F32, ph2) for i in range(NPSC)]
    ptp0 = ps("ptp0", (128, TPW), BF16, ph2)
    ptp1 = ps("ptp1", (128, TPW), BF16, ph2)
    ppv = ps("ppv", (128, G * 128), F32, ph2)
    ptb2 = ps("ptb2", (128, G * 128), BF16, ph2)
    dparts = sb("dparts", (128, 16 * G), F32, ph2)
    den = sb("den", (128, 2 * G), F32, ph2)
    biasv = sb("biasv", (128, 2 * G), F32, ph2)

    av = sb("av", (128, G * 128), BF16, ph2)
    ev_psc_free = {i: () for i in range(NPSC)}
    ev_ptp_free = {0: (), 1: ()}
    ev_den_gh = [None] * G
    ev_exp_gh = [None] * G
    ev_pT_free = ()
    ev_pbuf_free = {0: (), 1: ()}
    ev_ppv_free = ()
    ev_av_free = ()
    ev_ptb2_free = ()

    def ph2_head_meta(kvg, qt, gh):
        h = kvg * G + gh
        return {
            "kvg": kvg, "qt": qt, "gh": gh, "h": h,
            "dbase": 16 * gh, "bb": 2 * gh,
            "pbuf_i": (qt * G + gh) % 2,
            "c0": qt * 128,
            "bap": bounds_c[:, h * nqt + qt:h * nqt + qt + 1],
            "nexp": 0, "ev_exp": None, "ev_exp_q": [None] * nqr,
        }

    def ph2_emit_bias(m):
        wb = (ev_bounds, ev_consts)
        if ev_exp_gh[m["gh"]] is not None:
            wb = wb + (ev_exp_gh[m["gh"]],)
        bb = m["bb"]
        P.emit("vector", lambda e, bap=m["bap"], bb=bb: e.tensor_scalar(
            biasv[:, bb:bb + 1], bap, 1.0, None, ALU.add),
            waits=wb, incs=((s_dve, 1),))
        P.emit("vector", lambda e, bap=m["bap"], bb=bb: e.tensor_tensor(
            biasv[:, bb + 1:bb + 2], bap, hflag[:, 0:1], ALU.add),
            incs=((s_dve, 1),))
        m["ev_bias"] = s_dve.ev

    def ph2_stage_S(m, qr0, qr1):
        h, qt, kvg, c0, bb = m["h"], m["qt"], m["kvg"], m["c0"], m["bb"]
        pb = p_buf[:, m["pbuf_i"] * T_kv:(m["pbuf_i"] + 1) * T_kv]
        for qr in range(qr0, qr1):
            psc = pscs[qr % NPSC]
            lo = qr * QW
            has_diag = lo <= c0 < lo + QW
            w0 = (ev_qkv, ev_consts) + ev_psc_free[qr % NPSC]
            P.emit("tensor", lambda e, h=h, qt=qt, lo=lo, kvg=kvg,
                   psc=psc: e.matmul(
                psc[:, 0:QW],
                q_P[:, h * T_own + qt * 128:h * T_own + (qt + 1) * 128],
                k_P[:, kvg * T_kv + lo:kvg * T_kv + lo + QW],
                start=True, stop=not has_diag),
                waits=w0, incs=() if has_diag else ((s_pe, 1),))
            if has_diag:
                P.emit("tensor", lambda e, psc=psc, o=c0 - lo: e.matmul(
                    psc[:, o:o + 128], ident_b[:, :], tril[:, :],
                    start=False, stop=True),
                    incs=((s_pe, 1),))
            ev_sc = s_pe.ev
            spans = []
            for (x0, x1, bi) in ((0, c0, 1), (c0, T_kv // 2, 0),
                                 (T_kv // 2, T_kv, 2)):
                r0_, r1_ = max(x0, lo), min(x1, lo + QW)
                if r0_ < r1_:
                    spans.append((r0_, r1_, bi))
            for (r0_, r1_, bi) in spans:
                ci = m["dbase"] + m["nexp"]
                m["nexp"] += 1
                bias_ap = m["bap"] if bi == 0 else biasv[:, bb + bi - 1:bb + bi]
                wexp = (ev_sc,)
                if m["nexp"] == 1:
                    wexp = wexp + (m["ev_bias"],) + ev_pbuf_free[m["pbuf_i"]]
                    if ev_den_gh[m["gh"]] is not None:
                        wexp = wexp + (ev_den_gh[m["gh"]],)
                P.emit("scalar", lambda e, r0_=r0_, r1_=r1_, psc=psc,
                       lo=lo, ci=ci, pb=pb, bias_ap=bias_ap: e.activation(
                    pb[:, r0_:r1_], psc[:, r0_ - lo:r1_ - lo], AF.Exp,
                    bias=bias_ap, scale=scale,
                    accum_out=dparts[:, ci:ci + 1]),
                    waits=wexp, incs=((s_act, 1),))
            ev_psc_free[qr % NPSC] = (s_act.ev,)
            m["ev_exp_q"][qr] = s_act.ev
        if qr1 == nqr:
            m["ev_exp"] = s_act.ev
            ev_exp_gh[m["gh"]] = s_act.ev

    def ph2_stage_Tpre(m, ev_rd_gh):
        gh, dbase = m["gh"], m["dbase"]
        P.emit("vector", lambda e, nca=m["nexp"], gh=gh, dbase=dbase: e.reduce_sum(
            den[:, gh:gh + 1], dparts[:, dbase:dbase + nca], AX.X),
            waits=(m["ev_exp"],), incs=((s_dve, 1),))
        ev_den = s_dve.ev
        ev_den_gh[gh] = ev_den
        P.emit("vector", lambda e, gh=gh: e.reciprocal(
            den[:, G + gh:G + gh + 1], den[:, gh:gh + 1]),
            waits=(ev_den,), incs=((s_dve, 1),))
        ev_rd_gh[gh] = s_dve.ev

    def ph2_stage_Ttb(m, tb, ev_rd_gh, ev_pT_parts):
        gh = m["gh"]
        pb = p_buf[:, m["pbuf_i"] * T_kv:(m["pbuf_i"] + 1) * T_kv]
        ptp = ptp0 if tb % 2 == 0 else ptp1
        qhi = min((tb + 1) * (TPW // QW) - 1, nqr - 1)
        w1 = (m["ev_exp_q"][qhi],) + ev_ptp_free[tb % 2] + ev_pT_free
        for sc in range(TPW // 128):
            scg = tb * (TPW // 128) + sc
            P.emit("tensor", lambda e, scg=scg, sc=sc, ptp=ptp,
                   pb=pb: e.transpose(
                ptp[:, sc * 128:(sc + 1) * 128],
                pb[:, scg * 128:(scg + 1) * 128],
                ident_b[:, :]),
                waits=w1 if sc == 0 else (),
                incs=((s_pe, 1),) if sc == TPW // 128 - 1 else ())
        ev_tp = s_pe.ev
        P.emit("vector", lambda e, gh=gh, tb=tb, ptp=ptp: e.tensor_copy(
            pT[:, gh * T_kv + tb * TPW:gh * T_kv + (tb + 1) * TPW],
            ptp[:, :]),
            waits=(ev_tp,), incs=((s_dve, 1),))
        ev_ptp_free[tb % 2] = (s_dve.ev,)
        ev_pT_parts.append(s_dve.ev)
        if tb == T_kv // TPW - 1:
            ev_pbuf_free[m["pbuf_i"]] = (s_dve.ev,)

    pairs = [(kvg, qt) for kvg in range(KV) for qt in range(nqt)]
    ntpb = T_kv // TPW
    QPT = max(1, (nqr + ntpb - 1) // ntpb)
    pend = None
    for pi, (kvg, qt) in enumerate(pairs):
        hs = [kvg * G + gh for gh in range(G)]
        ev_rd_gh = [None] * G
        ev_pT_parts = []
        if pend is None:
            pend = ph2_head_meta(kvg, qt, 0)
            ph2_emit_bias(pend)
            ph2_stage_S(pend, 0, nqr)
        for gh in range(G):
            cur = pend
            nxt = None
            if gh + 1 < G:
                nxt = ph2_head_meta(kvg, qt, gh + 1)
            elif pi + 1 < len(pairs):
                nxt = ph2_head_meta(*pairs[pi + 1], 0)
            if nxt is not None:
                ph2_emit_bias(nxt)
            for tb in range(ntpb):
                if nxt is not None:
                    ph2_stage_S(nxt, min(tb * QPT, nqr),
                                min((tb + 1) * QPT, nqr) if tb + 1 < ntpb else nqr)
                ph2_stage_Ttb(cur, tb, ev_rd_gh, ev_pT_parts)
            ph2_stage_Tpre(cur, ev_rd_gh)
            pend = nxt
        ev_pT_ready = (ev_pT_parts[-1],)
        w0 = ev_pT_ready + ev_ppv_free
        for sc in range(nst):
            for gh in range(G):
                P.emit("tensor", lambda e, sc=sc, gh=gh, kvg=kvg,
                       st=(sc == 0 and gh == 0),
                       sp=(sc == nst - 1 and gh == G - 1): e.matmul(
                    ppv[:, gh * 128:(gh + 1) * 128],
                    pT[:, gh * T_kv + sc * 128:gh * T_kv + (sc + 1) * 128],
                    v_T[:, sc * KVD + kvg * 128:sc * KVD + (kvg + 1) * 128],
                    start=st, stop=sp, skip_group_check=True),
                    waits=w0 if sc == 0 and gh == 0 else (),
                    incs=((s_pe, 1),) if sc == nst - 1 and gh == G - 1 else ())
        ev_pv = s_pe.ev
        ev_pT_free = (ev_pv,)
        for gh in range(G):
            P.emit("vector", lambda e, gh=gh: e.tensor_scalar(
                av[:, gh * 128:(gh + 1) * 128],
                ppv[:, gh * 128:(gh + 1) * 128],
                den[:, G + gh:G + gh + 1], None, ALU.mult),
                waits=(ev_pv,) + ev_av_free if gh == 0 else (),
                incs=((s_dve, 1),) if gh == G - 1 else ())
        ev_av = s_dve.ev
        for gh in range(G):
            P.emit("tensor", lambda e, gh=gh: e.transpose(
                ptb2[:, gh * 128:(gh + 1) * 128],
                av[:, gh * 128:(gh + 1) * 128], ident_b[:, :]),
                waits=(ev_av,) + ev_ptb2_free if gh == 0 else (),
                incs=((s_pe, 1),) if gh == G - 1 else ())
        ev_tb2 = s_pe.ev
        ev_ppv_free = (ev_av,)
        for gh, h in enumerate(hs):
            P.emit("vector", lambda e, h=h, qt=qt, gh=gh: e.tensor_copy(
                q_P[:, h * T_own + qt * 128:h * T_own + (qt + 1) * 128],
                ptb2[:, gh * 128:(gh + 1) * 128]),
                waits=(ev_tb2,) if gh == 0 else (),
                incs=((s_dve, 1),) if gh == G - 1 else ())
        ev_ptb2_free = (s_dve.ev,)
        ev_av_free = (s_dve.ev,)
    ev_attnout = s_dve.ev
    ph2.close()
    P.phase_barrier()
    if stop_after == "2":
        return _stop()

    if "attn" in debug:
        dout("dbg_attn", (128, H * T_own))
        P.dma("gpsimd", dr["dbg_attn"], q_P[:, :], s_dbg, waits=(ev_attnout,))

    woT_s = at("woT_s", (128, DC * D), BF16, R1)
    xr = at("xr", (128, D), F32, R1 + DC * D * 2)
    x2_fm = at("x2_fm", (128, DC * TC), BF16, R2)
    rbuf = at("rbuf", (128, D), F32, R3)
    x2ff = at("x2ff", (128, D), F32, R3 + 4 * D)
    sqs = at("sqs", (128, D), BF16, R3 + 8 * D)
    ph3 = contextlib.ExitStack()
    stat = sb("stat", (128, 8), F32, ph3)
    lgf = sb("lgf", (EP1 if EP1 <= 128 else 128, min(512, T_own)), F32, ph3)
    py = ps("py", (128, D), F32, ph3)
    pgate = ps("pgate", (EP1, 512), F32, ph3)
    ptx2 = ps("ptx2", (128, min(1024, D)), F32, ph3)
    ptg = ps("ptg", (128, 4 * EP1), F32, ph3)
    GW = min(512, T_own)
    ngrp = GW // 128

    for fi in range(DC):
        P.dma("gpsimd", woT_s[:, fi * D:(fi + 1) * D], wslab_ap("woT", fi, D),
              s_wo)
    ev_wo = (s_wo, s_wo.n)

    ev_py_free = ()
    ev_xr_free = ()
    ev_r_free = ()
    ev_x2ff_free = ()
    ev_gate_ev = ()
    ev_ptx2_free = ()
    ev_ptg_free = ()
    ev_spill = []
    ev_x2fm_part = []
    rt = rbuf[:, 0:D]
    for tt in range(nqt):
        ev_xr = P.dma("sync", xr[:, :], dr["x_kv"][tt * 128:(tt + 1) * 128, :],
                      s_xr, waits=ev_xr_free)
        w0 = (ev_attnout, ev_wo, ev_ones, ev_brows) + ev_py_free
        for fi in range(DC):
            for fg in range(FOG):
                P.emit("tensor", lambda e, fi=fi, fg=fg, tt=tt: e.matmul(
                    py[:, fg * FW:(fg + 1) * FW],
                    q_P[:, fi * T_own + tt * 128:fi * T_own + (tt + 1) * 128],
                    woT_s[:, fi * D + fg * FW:fi * D + (fg + 1) * FW],
                    start=(fi == 0), stop=False,
                    skip_group_check=True),
                    waits=w0 if fi == 0 and fg == 0 else ())
        for fg in range(FOG):
            P.emit("tensor", lambda e, fg=fg: e.matmul(
                py[:, fg * FW:(fg + 1) * FW],
                ones_b[0:1, 0:128], bo_row[0:1, fg * FW:(fg + 1) * FW],
                start=False, stop=True, skip_group_check=True),
                incs=((s_pe, 1),) if fg == FOG - 1 else ())
        ev_y = s_pe.ev
        for fg in range(FOG):
            P.emit("vector", lambda e, fg=fg: e.tensor_add(
                rt[:, fg * FW:(fg + 1) * FW], py[:, fg * FW:(fg + 1) * FW],
                xr[:, fg * FW:(fg + 1) * FW]),
                waits=(ev_y, ev_xr) + ev_r_free if fg == 0 else (),
                incs=((s_dve, 1),) if fg == FOG - 1 else ())
        ev_r = s_dve.ev
        ev_py_free = (ev_r,)
        ev_xr_free = (ev_r,)
        P.emit("scalar", lambda e: e.activation(
            sqs[:, :], rt, AF.Square, accum_out=stat[:, 1:2]),
            waits=(ev_r,), incs=((s_act, 1),))
        ev_sq = s_act.ev
        P.emit("vector", lambda e: e.reduce_sum(stat[:, 0:1], rt, AX.X),
               waits=(ev_r,), incs=((s_dve, 1),))
        ev_sum = s_dve.ev
        P.emit("vector", lambda e: e.tensor_scalar(
            stat[:, 2:3], stat[:, 0:1], 1.0 / D, None, ALU.mult),
            waits=(ev_sum,), incs=((s_dve, 1),))
        ev_mean = s_dve.ev
        P.emit("vector", lambda e: e.tensor_scalar(
            stat[:, 3:4], stat[:, 1:2], 1.0 / D, None, ALU.mult),
            waits=(ev_sq,), incs=((s_dve, 1),))
        P.emit("vector", lambda e: e.tensor_tensor(
            stat[:, 4:5], stat[:, 2:3], stat[:, 2:3], ALU.mult),
            waits=(ev_mean,), incs=((s_dve, 1),))
        ev_m2 = s_dve.ev
        P.emit("vector", lambda e: e.tensor_tensor(
            stat[:, 5:6], stat[:, 3:4], stat[:, 4:5], ALU.subtract),
            waits=(ev_m2,), incs=((s_dve, 1),))
        ev_var = s_dve.ev
        P.emit("scalar", lambda e: e.activation(
            stat[:, 6:7], stat[:, 5:6], AF.Ln, bias=eps_c[:, 0:1]),
            waits=(ev_var,), incs=((s_act, 1),))
        ev_ln = s_act.ev
        P.emit("scalar", lambda e: e.activation(
            stat[:, 7:8], stat[:, 6:7], AF.Exp, scale=-0.5),
            waits=(ev_ln,), incs=((s_act, 1),))
        ev_rstd = s_act.ev
        P.emit("vector", lambda e: e.tensor_scalar(
            rt, rt, stat[:, 2:3], stat[:, 7:8], ALU.subtract, ALU.mult),
            waits=(ev_rstd,), incs=((s_dve, 1),))
        ev_x2 = s_dve.ev
        if "spill" not in ph3_skip:
            evsp = P.dma("sync", dr["x2sp"][tt * 128:(tt + 1) * 128, :], rt,
                         s_sp[tt % 2], waits=(ev_x2,))
            ev_spill.append(evsp)
        else:
            evsp = ev_x2
        if "tx" in ph3_skip:
            ev_r_free = (evsp,)
            continue
        np_ = min(1024, D) // 128
        ev_tx_evs = []
        ev_tx_last = None
        for phx in range((DC + np_ - 1) // np_):
            fis = list(range(phx * np_, min(DC, (phx + 1) * np_)))
            w1 = (ev_x2,) + ev_ptx2_free
            for k_, fi in enumerate(fis):
                P.emit("tensor", lambda e, fi=fi, k_=k_: e.transpose(
                    ptx2[:, k_ * 128:(k_ + 1) * 128],
                    rt[:, fi * 128:(fi + 1) * 128], ident_f[:, :]),
                    waits=w1 if k_ == 0 else (),
                    incs=((s_pe, 1),) if k_ == len(fis) - 1 else ())
            ev_tx = s_pe.ev
            ev_tx_last = ev_tx
            ev_ptx2_free = (ev_tx,)
            if "txv" not in ph3_skip:
                P.emit("vector", lambda e, phx=phx, np_=np_, n=len(fis): e.tensor_copy(
                    x2ff[:, phx * np_ * 128:phx * np_ * 128 + n * 128],
                    ptx2[:, 0:n * 128]),
                    waits=(ev_tx,) + ev_x2ff_free, incs=((s_dve, 1),))
                ev_tx_evs.append(s_dve.ev)
                ev_ptx2_free = (s_dve.ev,)
            if "txs" in ph3_skip:
                pass
            elif tt < ntt:
                ev_vc = ev_ptx2_free
                for k_, fi in enumerate(fis):
                    P.emit("scalar", lambda e, k_=k_, fi=fi, tt=tt: e.copy(
                        x2_fm[:, fi * TC + tt * 128:fi * TC + (tt + 1) * 128],
                        ptx2[:, k_ * 128:(k_ + 1) * 128]),
                        waits=((ev_tx,) + ev_vc) if k_ == 0 else (),
                        incs=((s_act, 1),) if k_ == len(fis) - 1 else ())
                ev_ptx2_free = ev_ptx2_free + (s_act.ev,)

        ev_x2ff = ev_tx_evs[-1] if ev_tx_evs else None
        if tt < ntt and "txs" not in ph3_skip:
            ev_x2fm_part.append(s_act.ev)
        ev_spf_tt = ()
        if tt >= ntt and "txv" not in ph3_skip:
            ch = tt // ntt
            off = (ch - 1) * DC * TC + (tt % ntt) * 128
            dst = bass.AP(dr["x2fmsp"].tensor, off,
                          [[(nch - 1) * DC * TC, 128], [TC, DC], [1, 128]])
            evspf = P.dma("gpsimd", dst, x2ff[:, 0:D], s_x2f,
                          waits=(ev_x2ff,))
            ev_spf_tt = (evspf,)
            ev_x2ff_free = ev_x2ff_free + ev_spf_tt
        ev_spf_all = (s_x2f, s_x2f.n)
        ev_r_free = (evsp, ev_tx_last)
        if "gate" in ph3_skip:
            ev_x2ff_free = (ev_x2ff,)
            continue
        gi = tt % ngrp
        for fi in range(DC):
            P.emit("tensor", lambda e, fi=fi, gi=gi,
                   st=(gi == 0 and fi == 0): e.matmul(
                pgate[:, gi * 128:(gi + 1) * 128],
                gseg_s[:, fi * EP1:(fi + 1) * EP1],
                x2ff[:, fi * 128:(fi + 1) * 128],
                start=st, stop=(fi == DC - 1), skip_group_check=True),
                waits=(ev_x2ff, ev_consts) + ev_gate_ev if fi == 0 else (),
                incs=((s_pe, 1),) if (fi == DC - 1) else ())
        ev_gmm = s_pe.ev
        ev_x2ff_free = (ev_gmm,) + ev_spf_tt
        if gi == ngrp - 1:
            P.emit("vector", lambda e: e.tensor_copy(lgf[:, :], pgate[0:EP1, 0:GW]),
                   waits=(ev_gmm,) + ev_ptg_free, incs=((s_dve, 1),))
            ev_lg = s_dve.ev
            ev_gate_ev = (ev_lg,)
            for g2 in range(ngrp):
                P.emit("tensor", lambda e, g2=g2: e.transpose(
                    ptg[:, g2 * EP1:(g2 + 1) * EP1],
                    lgf[0:EP1, g2 * 128:(g2 + 1) * 128],
                    ident_f[0:EP1, 0:EP1]),
                    waits=(ev_lg, ev_consts) + ev_ptg_free if g2 == 0 else (),
                    incs=((s_pe, 1),) if g2 == ngrp - 1 else ())
            ev_tg = s_pe.ev
            P.emit("vector", lambda e, tt=tt: e.tensor_copy(
                ltma[:, (tt - ngrp + 1) * EP1:(tt + 1) * EP1],
                ptg[:, 0:ngrp * EP1]),
                waits=(ev_tg,), incs=((s_dve, 1),))
            ev_ptg_free = (s_dve.ev,)
    ev_ln1_done = s_dve.ev
    ph3.close()
    P.phase_barrier()
    if stop_after == "3":
        return _stop()

    if "ln1" in debug:
        dout("dbg_x2", (T_own, D))
        for tt in range(nqt):
            P.dma("sync", dr["dbg_x2"][tt * 128:(tt + 1) * 128, :],
                  dr["x2sp"][tt * 128:(tt + 1) * 128, :], s_st[0],
                  waits=(s_sp[0].ev, s_sp[1].ev))
        dout("dbg_logits", (128, nqt * EP1))
        P.dma("sync", dr["dbg_logits"], ltma[:, :], s_st[0],
              waits=(ev_ln1_done,))

    acc = at("acc", (128, ntt * D), F32, R0)
    wring = at("wring", (128, NSLOT * SLOTW), BF16, R1)
    h_buf = at("h_buf", (128, max(FC, PFC) * TC), BF16, R3)
    sqs2 = at("sqs2", (128, D), BF16, R3)
    xr2 = at("xr2", (128, D), F32, R3 + 2 * D)
    sact = at("sact", (128, TC), BF16, R3 + max(FC, PFC) * TC * 2 + 2 * D)
    ph4 = contextlib.ExitStack()
    comb = sb("comb", (128, ntt * E), F32, ph4)
    mbuf = sb("mbuf", (128, Ktop * E), F32, ph4)
    curb = sb("curb", (128, E), F32, ph4)
    mvals = sb("mvals", (128, 8), F32, ph4)
    sigc = sb("sigc", (128, ntt), F32, ph4)
    stat2 = sb("stat2", (128, 8), F32, ph4)

    slot_free = [()] * NSLOT
    slot_i = [0]

    def load_unit(src_ap, width, waits=()):
        i = slot_i[0] % NSLOT
        slot_i[0] += 1
        sl = wring[:, i * SLOTW:i * SLOTW + width]
        ev = P.dma("gpsimd", sl, src_ap, s_w[i],
                   waits=tuple(waits) + slot_free[i])
        return i, sl, ev

    pg = ps("pg", (128, TC), F32, ph4)
    pu = ps("pu", (128, TC), F32, ph4)
    pdl = [ps("pd%d" % i, (128, FW), F32, ph4) for i in range(FOG)]
    st_ev_tt = {}
    gu_last = [None]
    for c in range(nch):
        t0 = c * TC
        ev_sp_all = (s_sp[0].ev, s_sp[1].ev)
        if c > 0:
            evld = P.dma("sync",
                         x2_fm[:, :],
                         dr["x2fmsp"][:, (c - 1) * DC * TC:c * DC * TC],
                         s_x2r, waits=(gu_last[0], ev_spf_all))
            ev_x2fm = (evld,)
        else:
            ev_x2fm = tuple(ev_x2fm_part[-1:])

        ev_comb = []
        for tt in range(ntt):
            ttg = c * ntt + tt
            lsl = ltma[:, ttg * EP1:ttg * EP1 + E]
            P.emit("vector", lambda e, lsl=lsl: e.tensor_copy(curb[:, :], lsl),
                   waits=(ev_ln1_done,), incs=((s_dve, 1),))
            ev_cur = s_dve.ev
            ev_mask = None
            for j in range(Ktop):
                P.emit("vector", lambda e, j=j: e.reduce_max(
                    mvals[:, j:j + 1], curb[:, :], AX.X),
                    waits=(ev_cur,), incs=((s_dve, 1),))
                ev_mj = s_dve.ev
                P.emit("vector", lambda e, j=j: e.tensor_scalar(
                    mbuf[:, j * E:(j + 1) * E], curb[:, :], mvals[:, j:j + 1],
                    None, ALU.is_equal),
                    waits=(ev_mj,), incs=((s_dve, 1),))
                ev_mask = s_dve.ev
                if j < Ktop - 1:
                    P.emit("vector", lambda e, j=j: e.scalar_tensor_tensor(
                        curb[:, :], mbuf[:, j * E:(j + 1) * E], -1e30,
                        curb[:, :], ALU.mult, ALU.add),
                        waits=(ev_mask,), incs=((s_dve, 1),))
                    ev_cur = s_dve.ev
            P.emit("vector", lambda e: e.tensor_scalar(
                mvals[:, 4:5], mvals[:, 0:1], -1.0, None, ALU.mult),
                waits=(ev_mask,), incs=((s_dve, 1),))
            ev_neg = s_dve.ev
            P.emit("scalar", lambda e, K=Ktop: e.activation(
                mvals[:, 0:K], mvals[:, 0:K], AF.Exp, bias=mvals[:, 4:5],
                accum_out=mvals[:, 5:6]),
                waits=(ev_neg,), incs=((s_act, 1),))
            ev_ex = s_act.ev
            P.emit("vector", lambda e: e.reciprocal(mvals[:, 6:7], mvals[:, 5:6]),
                   waits=(ev_ex,), incs=((s_dve, 1),))
            ev_rc = s_dve.ev
            P.emit("vector", lambda e, K=Ktop: e.tensor_scalar(
                mvals[:, 0:K], mvals[:, 0:K], mvals[:, 6:7], None, ALU.mult),
                waits=(ev_rc,), incs=((s_dve, 1),))
            ev_cv = s_dve.ev
            for j in range(Ktop):
                if j == 0:
                    P.emit("vector", lambda e, tt=tt: e.tensor_scalar(
                        comb[:, tt * E:(tt + 1) * E], mbuf[:, 0:E],
                        mvals[:, 0:1], None, ALU.mult),
                        waits=(ev_cv,), incs=((s_dve, 1),))
                else:
                    P.emit("vector", lambda e, tt=tt, j=j: e.scalar_tensor_tensor(
                        comb[:, tt * E:(tt + 1) * E], mbuf[:, j * E:(j + 1) * E],
                        mvals[:, j:j + 1], comb[:, tt * E:(tt + 1) * E],
                        ALU.mult, ALU.add),
                        waits=(s_dve.ev,), incs=((s_dve, 1),))
            ev_comb.append(s_dve.ev)
            P.emit("scalar", lambda e, ttg=ttg: e.activation(
                mvals[:, 7:8], ltma[:, ttg * EP1 + E:ttg * EP1 + E + 1], AF.Exp,
                scale=-1.0),
                waits=(ev_ln1_done,), incs=((s_act, 1),))
            ev_sg = s_act.ev
            P.emit("vector", lambda e: e.tensor_scalar(
                mvals[:, 7:8], mvals[:, 7:8], 1.0, None, ALU.add),
                waits=(ev_sg,), incs=((s_dve, 1),))
            ev_sg1 = s_dve.ev
            P.emit("vector", lambda e, tt=tt: e.reciprocal(
                sigc[:, tt:tt + 1], mvals[:, 7:8]),
                waits=(ev_sg1,), incs=((s_dve, 1),))
        ev_gatep = s_dve.ev

        ev_pg_free = ()
        ev_pu_free = ()
        ev_pd_free = [()] * FOG
        ev_acc = {}

        def expert(kind, ei, first):
            FCur = PF if kind == "pse" else F
            FCc = PFC if kind == "pse" else FC
            nonlocal ev_pg_free, ev_pu_free
            hF = (FCc + 1) // 2

            def gu_ap(dname, hh):
                t = dr[dname].tensor
                base = ei * D * FCur + hh * hD * 128 * FCur
                return bass.AP(t, base, [[FCur, 128], [128 * FCur, hD], [1, FCur]])

            def d_ap(lo, hi):
                nsl = hi - lo
                if kind == "pse":
                    t = dr["sd"].tensor
                    base = (ei * PF + lo * 128) * D
                else:
                    t = dr["wd"].tensor
                    base = ei * F * D + lo * 128 * D
                return bass.AP(t, base, [[D, 128], [128 * D, nsl], [1, D]])

            gnm = "sgb" if kind == "pse" else "wg"
            unm = "sub" if kind == "pse" else "wu"
            sl_g = [load_unit(gu_ap(gnm, hh), hD * FCur) for hh in range(2)]
            sl_u = [load_unit(gu_ap(unm, hh), hD * FCur) for hh in range(2)]
            sl_d = []
            for hh in range(2):
                lo = hh * hF
                nsl = min(FCc, (hh + 1) * hF) - lo
                if nsl > 0:
                    sl_d.append((load_unit(d_ap(lo, lo + nsl), nsl * D), lo, nsl))

            def scol_ap(tt):
                if kind == "pse":
                    return sigc[:, tt:tt + 1]
                return comb[:, tt * E + ei:tt * E + ei + 1]

            def gu_mms(dst_psum, slots, ev_free, fo):
                for fi in range(DC):
                    ui = fi // hD
                    si, sl, evl = slots[ui]
                    col = (fi % hD) * FCur + fo * 128
                    w = ()
                    if fi == 0:
                        w = (evl,) + ev_x2fm + ev_free
                    elif fi % hD == 0:
                        w = (evl,)
                    for nsx in range(nfsub):
                        P.emit("tensor", lambda e, sl=sl, col=col, fi=fi,
                               nsx=nsx, dst=dst_psum: e.matmul(
                            dst[:, nsx * NF:(nsx + 1) * NF],
                            sl[:, col:col + 128],
                            x2_fm[:, fi * TC + nsx * NF:fi * TC + (nsx + 1) * NF],
                            start=(fi == 0), stop=(fi == DC - 1)),
                            waits=w if nsx == 0 else (),
                            incs=((s_pe, 1),)
                                 if fi == DC - 1 and nsx == nfsub - 1 else ())
                return s_pe.ev

            ev_hall = None
            for fo in range(FCc):
                ev_g = gu_mms(pg, sl_g, ev_pg_free, fo)
                if fo == FCc - 1:
                    for (si, sl, evl) in sl_g:
                        slot_free[si] = (ev_g,)
                P.emit("scalar", lambda e, pg=pg: e.activation(
                    sact[:, :], pg[:, :], AF.Silu),
                    waits=(ev_g,) + ev_pu_free, incs=((s_act, 1),))
                ev_si = s_act.ev
                ev_pg_free = (ev_si,)
                ev_u = gu_mms(pu, sl_u, ev_pu_free, fo)
                if fo == FCc - 1:
                    for (si, sl, evl) in sl_u:
                        slot_free[si] = (ev_u,)
                P.emit("vector", lambda e, fo=fo, pu=pu: e.tensor_tensor(
                    h_buf[:, fo * TC:(fo + 1) * TC], pu[:, :], sact[:, :],
                    ALU.mult),
                    waits=(ev_u, ev_si), incs=((s_dve, 1),))
                ev_pu_free = (s_dve.ev,)
                ev_hall = s_dve.ev
                gu_last[0] = ev_u
            for tt in range(ntt):
                ev_d_fg = [None] * FOG
                for fg in range(FOG):
                    for (slot, lo, nsl) in sl_d:
                        si, sl, evl = slot
                        for k_ in range(nsl):
                            f = lo + k_
                            w = ()
                            if fg == 0 and k_ == 0:
                                w = (evl, ev_hall)
                            if f == 0:
                                w = w + ev_pd_free[fg]
                            P.emit("tensor", lambda e, sl=sl, k_=k_, f=f, fg=fg,
                                   tt=tt, pd=pdl[fg]: e.matmul(
                                pd[:, 0:FW],
                                h_buf[:, f * TC + tt * 128:f * TC + (tt + 1) * 128],
                                sl[:, k_ * D + fg * FW:k_ * D + (fg + 1) * FW],
                                start=(f == 0), stop=(f == FCc - 1)),
                                waits=w,
                                incs=((s_pe, 1),) if f == FCc - 1 else ())
                    ev_d_fg[fg] = s_pe.ev
                    a = acc[:, tt * D + fg * FW:tt * D + (fg + 1) * FW]
                    if first:
                        P.emit("vector", lambda e, a=a, pd=pdl[fg],
                               sc_=scol_ap(tt): e.tensor_scalar(
                            a, pd[:, 0:FW], sc_, None, ALU.mult),
                            waits=(ev_d_fg[fg], ev_gatep) + st_ev_tt.get(tt, ()),
                            incs=((s_dve, 1),))
                    else:
                        P.emit("vector", lambda e, a=a, pd=pdl[fg],
                               sc_=scol_ap(tt): e.scalar_tensor_tensor(
                            a, pd[:, 0:FW], sc_, a, ALU.mult, ALU.add),
                            waits=(ev_d_fg[fg], ev_gatep), incs=((s_dve, 1),))
                    ev_pd_free[fg] = (s_dve.ev,)
                ev_d = s_pe.ev
                if tt == ntt - 1:
                    for (slot, lo, nsl) in sl_d:
                        slot_free[slot[0]] = (ev_d,)
                ev_acc[tt] = (s_dve.ev,)

        for ei in range(NPSE):
            expert("pse", ei, first=(ei == 0))
        for ei in range(E):
            expert("moe", ei, first=False)

        ev_pe_now = (s_pe, s_pe.n)
        ev_xr2_free = ()
        for tt in range(ntt):
            pi = tt % 2
            xt = xr2[:, :]
            ev_x2r = P.dma("sync", xt,
                           dr["x2sp"][t0 + tt * 128:t0 + (tt + 1) * 128, :],
                           s_x2l[pi],
                           waits=ev_sp_all + (ev_pe_now,) + ev_xr2_free)
            a = acc[:, tt * D:(tt + 1) * D]
            P.emit("vector", lambda e, a=a, xt=xt: e.tensor_add(a, a, xt),
                   waits=(ev_x2r,) + ev_acc.get(tt, ()), incs=((s_dve, 1),))
            ev_r2 = s_dve.ev
            ev_xr2_free = (ev_r2,)
            P.emit("scalar", lambda e, a=a: e.activation(
                sqs2[:, :], a, AF.Square, accum_out=stat2[:, 1:2]),
                waits=(ev_r2, ev_pe_now), incs=((s_act, 1),))
            ev_sq = s_act.ev
            P.emit("vector", lambda e, a=a: e.reduce_sum(stat2[:, 0:1], a, AX.X),
                   waits=(ev_r2,), incs=((s_dve, 1),))
            ev_sum = s_dve.ev
            P.emit("vector", lambda e: e.tensor_scalar(
                stat2[:, 2:3], stat2[:, 0:1], 1.0 / D, None, ALU.mult),
                waits=(ev_sum,), incs=((s_dve, 1),))
            ev_mean = s_dve.ev
            P.emit("vector", lambda e: e.tensor_scalar(
                stat2[:, 3:4], stat2[:, 1:2], 1.0 / D, None, ALU.mult),
                waits=(ev_sq,), incs=((s_dve, 1),))
            P.emit("vector", lambda e: e.tensor_tensor(
                stat2[:, 4:5], stat2[:, 2:3], stat2[:, 2:3], ALU.mult),
                waits=(ev_mean,), incs=((s_dve, 1),))
            ev_m2 = s_dve.ev
            P.emit("vector", lambda e: e.tensor_tensor(
                stat2[:, 5:6], stat2[:, 3:4], stat2[:, 4:5], ALU.subtract),
                waits=(ev_m2,), incs=((s_dve, 1),))
            ev_var = s_dve.ev
            P.emit("scalar", lambda e: e.activation(
                stat2[:, 6:7], stat2[:, 5:6], AF.Ln, bias=eps_c[:, 0:1]),
                waits=(ev_var,), incs=((s_act, 1),))
            ev_lnl = s_act.ev
            P.emit("scalar", lambda e: e.activation(
                stat2[:, 7:8], stat2[:, 6:7], AF.Exp, scale=-0.5),
                waits=(ev_lnl,), incs=((s_act, 1),))
            ev_rstd = s_act.ev
            P.emit("vector", lambda e, a=a: e.tensor_scalar(
                a, a, stat2[:, 2:3], stat2[:, 7:8], ALU.subtract, ALU.mult),
                waits=(ev_rstd, ev_pe_now), incs=((s_dve, 1),))
            ev_o = s_dve.ev
            st_ev_tt[tt] = (P.dma(
                "sync", dr["out"][t0 + tt * 128:t0 + (tt + 1) * 128, :], a,
                s_st[pi], waits=(ev_o,)),)
    ph4.close()
    P.emit("sync", lambda e: e.nop(),
           waits=((s_st[0], s_st[0].n), (s_st[1], s_st[1].n),
                  (s_dbg, s_dbg.n)))

    P.replay()
    es.close()
    return nc, dr


def kernel(**inputs):
    cfg = full_cfg()
    nc, dr = build_program(cfg)
    in_maps = [prep_core_inputs(inputs, c, cfg) for c in range(N_CORES)]
    res = run_bass_kernel_spmd(nc, in_maps, core_ids=list(range(N_CORES)))
    T_own, D = cfg["T_own"], cfg["D"]
    out = np.empty((cfg["B"], cfg["T_kv"], D), dtype=np.float32)
    for c in range(N_CORES):
        b, half = c // 2, c % 2
        out[b, half * T_own:(half + 1) * T_own, :] = res.results[c]["out"]
    return out
